# version 16 (final)
# baseline (speedup 1.0000x reference)
from contextlib import ExitStack

import numpy as np
import concourse.bass as bass
import concourse.mybir as mybir
from concourse.bass_core_rust import IndirectOffsetOnAxis
from concourse.bass_utils import run_bass_kernel_spmd

F32 = mybir.dt.float32
BF16 = mybir.dt.bfloat16
I32 = mybir.dt.int32
AF = mybir.ActivationFunctionType
ALU = mybir.AluOpType
AX = mybir.AxisListType

W = 8
B = 2
S = 8192
H = 2048
TOK = B * S
CH = TOK // W
SCALE = 1.0 / float(np.sqrt(np.float32(128.0)))
CMARG = 1.0202

ENGINES = ("sync", "vector", "scalar", "tensor", "gpsimd")


class KB:

    def __init__(self, nc):
        self.nc = nc
        self.q = {e: [] for e in ENGINES}

    def mksem(self, stack, name):
        h = stack.enter_context(self.nc.semaphore(name))
        return {"h": h, "n": 0, "name": name}

    def on(self, eng, fn):
        self.q[eng].append(fn)

    def wait(self, eng, sem, n):
        if n <= 0:
            return
        h = sem["h"]
        self.on(eng, lambda e, h=h, n=n: e.wait_ge(h, n))

    def op(self, eng, fn, mark=None):
        if mark is None:
            self.on(eng, fn)
            return None
        mark["n"] += 1
        h, tgt = mark["h"], mark["n"]
        self.on(eng, lambda e, fn=fn, h=h: fn(e).then_inc(h, 1))
        return tgt

    def dma(self, eng, out, in_, sem, transpose=False):
        sem["n"] += 16
        h, tgt = sem["h"], sem["n"]

        def fn(e, out=out, in_=in_, h=h):
            if transpose:
                return e.dma_start_transpose(out, in_).then_inc(h, 16)
            return e.dma_start(out=out, in_=in_).then_inc(h, 16)

        self.on(eng, fn)
        return tgt

    def idma(self, out, in_, off_ap, sem):
        sem["n"] += 16
        h, tgt = sem["h"], sem["n"]
        self.on("gpsimd", lambda e, out=out, in_=in_, off=off_ap, h=h:
                e.indirect_dma_start(
                    out=out, out_offset=None, in_=in_,
                    in_offset=IndirectOffsetOnAxis(off, axis=0)).then_inc(h, 16))
        return tgt

    def barrier(self, sem):
        sem["n"] += len(ENGINES)
        tgt, h = sem["n"], sem["h"]
        for eng in ENGINES:
            self.on(eng, lambda e, h=h: e.nop().then_inc(h, 1))
            self.on(eng, lambda e, h=h, t=tgt: e.wait_ge(h, t))

    def emit(self, blk):
        for eng in ENGINES:
            fns = self.q[eng]

            def body(e, fns=fns):
                for f in fns:
                    f(e)

            getattr(blk, eng)(body)


def build(debug=False):
    nc = bass.Bass("TRN2", target_bir_lowering=False)

    x_in = nc.dram_tensor("x", [TOK, H], F32, kind="ExternalInput").ap()
    x_own = nc.dram_tensor("x_own", [CH, H], F32, kind="ExternalInput").ap()
    x_own_t = nc.dram_tensor("x_own_t", [H, CH], F32, kind="ExternalInput").ap()
    wq_t = nc.dram_tensor("wq_t", [H, 768], F32, kind="ExternalInput").ap()
    wo_t = nc.dram_tensor("wo_t", [H, H], F32, kind="ExternalInput").ap()
    wf_t = nc.dram_tensor("wf_t", [H, H], F32, kind="ExternalInput").ap()
    fb_in = nc.dram_tensor("fb", [128, 16], F32, kind="ExternalInput").ap()
    gidx = nc.dram_tensor("gidx", [128, 16, 8], I32, kind="ExternalInput").ap()
    y_t = nc.dram_tensor("y_t", [H, CH], F32, kind="ExternalOutput").ap()

    xn_d = nc.dram_tensor("xn_d", [CH, H], BF16, kind="Internal").ap()
    xn_full = nc.dram_tensor("xn_full", [TOK, H], BF16, kind="Internal").ap()
    g_d = nc.dram_tensor("g_d", [H, CH], BF16, kind="Internal").ap()
    at_d = nc.dram_tensor("at_d", [TOK, 256], BF16, kind="Internal").ap()
    at_all = nc.dram_tensor("at_all", [2 * W * S, 256], BF16, kind="Internal").ap()
    fl_in = nc.dram_tensor("fl_in", [128, 4], I32, kind="Internal").ap()
    fl_out = nc.dram_tensor("fl_out", [128, 4], I32, kind="Internal").ap()

    kb = KB(nc)
    st = ExitStack()

    dS = kb.mksem(st, "dS")
    dC = kb.mksem(st, "dC")
    dG = kb.mksem(st, "dG")
    cc = kb.mksem(st, "cc")
    pT = kb.mksem(st, "pT")
    pV = kb.mksem(st, "pV")
    pC = kb.mksem(st, "pC")
    pP = kb.mksem(st, "pP")
    bar = kb.mksem(st, "bar")

    idb = st.enter_context(nc.sbuf_tensor("idb", [128, 128], BF16))
    onesb = st.enter_context(nc.sbuf_tensor("onesb", [128, 128], BF16))
    idio = st.enter_context(nc.sbuf_tensor("idio", [128, 128], F32))
    epsb = st.enter_context(nc.sbuf_tensor("epsb", [128, 1], F32))
    gix = st.enter_context(nc.sbuf_tensor("gix", [128, 16, 8], I32))
    fbs = st.enter_context(nc.sbuf_tensor("fbs", [128, 16], F32))

    t_gix = kb.dma("sync", gix[:], gidx, dS)
    kb.dma("sync", fbs[:], fb_in, dS)
    kb.op("gpsimd", lambda e: e.iota(
        idio[:], pattern=[[1, 128]], base=0, channel_multiplier=-1,
        allow_small_or_imprecise_dtypes=True), mark=pP)
    kb.wait("vector", pP, pP["n"])
    kb.op("vector", lambda e: e.tensor_single_scalar(idb[:], idio[:], 0.0, ALU.is_equal))
    kb.op("vector", lambda e: e.memset(onesb[:], 1.0))
    m_eps = kb.op("vector", lambda e: e.memset(epsb[:], 1e-5), mark=pV)
    kb.wait("sync", pV, m_eps)
    kb.dma("sync", fl_in, gix.rearrange("p a b -> p (a b)")[:, 0:4], dS)

    p3 = ExitStack()
    wqkvT = p3.enter_context(nc.sbuf_tensor("wqkvT", [128, 16, 768], BF16))
    t_wq = kb.dma("gpsimd", wqkvT[:],
                  wq_t.rearrange("(ic p) o -> p ic o", p=128), dG)

    lnbufs = p3.enter_context(nc.sbuf_tensor("xof", [128, 4, H], BF16))
    xnb = p3.enter_context(nc.sbuf_tensor("xnb", [128, 2, H], BF16))
    bnst = p3.enter_context(nc.sbuf_tensor("bnst", [128, 24], F32))
    bnagp = p3.enter_context(nc.sbuf_tensor("bnagp", [128, 2, 2], F32))
    lnstd = p3.enter_context(nc.sbuf_tensor("lnstd", [128, 2], F32))
    lnr = p3.enter_context(nc.sbuf_tensor("lnr", [128, 2], F32))

    ln_state = {"m_sq": [0, 0], "m_aff2": [0, 0], "m_aff4": [0, 0, 0, 0],
                "xn_dma": [0, 0], "t_x": [0, 0, 0, 0]}
    xn_tile_w = [0] * 128
    xnd_tile_w = [0] * 16

    ln_jobs = ([("g", g) for g in range(64)] + [("o", t) for t in range(16)]
               + [("g", g) for g in range(64, 128)])

    def ln_load(t):
        kind, idx = ln_jobs[t]
        p4 = t % 4
        src = (x_in if kind == "g" else x_own)[idx * 128:(idx + 2) * 128, :]
        kb.wait("gpsimd", pV, ln_state["m_aff4"][p4])
        kb.wait("gpsimd", pV, ln_state["m_aff4"][p4 + 1])
        tx = kb.dma("gpsimd", lnbufs[:, p4:p4 + 2, :],
                    src.rearrange("(tt p) n -> p tt n", p=128), dG)
        ln_state["t_x"][p4] = tx
        ln_state["t_x"][p4 + 1] = tx

    def ln_stats(t):
        p4, p2 = t % 4, t % 2
        kb.wait("vector", dG, ln_state["t_x"][p4])
        kb.wait("vector", pV, ln_state["m_aff2"][p2])
        for c4 in range(3):
            kb.op("vector", lambda e, p4=p4, c4=c4: e.bn_stats(
                bnst[:, c4 * 6:(c4 + 1) * 6],
                lnbufs[:, p4, c4 * 512:(c4 + 1) * 512]))
        m_st = kb.op("vector", lambda e, p4=p4: e.bn_stats(
            bnst[:, 18:24], lnbufs[:, p4, 1536:2048]), mark=pV)
        kb.wait("vector", pV, m_st)
        m_bn = kb.op("vector", lambda e, p2=p2: e.bn_aggr(
            bnagp[:, p2, :], bnst[:]), mark=pV)
        kb.wait("scalar", pV, m_bn)
        ln_state["m_sq"][p2] = kb.op("scalar", lambda e, p2=p2: e.activation(
            lnstd[:, p2:p2 + 1], bnagp[:, p2, 1:2], AF.Sqrt,
            bias=epsb[:], scale=1.0), mark=pC)

    def ln_fin(t):
        kind, idx = ln_jobs[t]
        p4, p2 = t % 4, t % 2
        dst = (xn_full if kind == "g" else xn_d)[idx * 128:(idx + 1) * 128, :]
        kb.wait("vector", pC, ln_state["m_sq"][p2])
        m_rc = kb.op("vector", lambda e, p2=p2: e.reciprocal(
            lnr[:, p2:p2 + 1], lnstd[:, p2:p2 + 1]), mark=pV)
        kb.wait("vector", pV, m_rc)
        kb.wait("vector", dG, ln_state["xn_dma"][p2])
        m_aff = kb.op("vector", lambda e, p4=p4, p2=p2: e.tensor_scalar(
            xnb[:, p2, :], lnbufs[:, p4, :], bnagp[:, p2, 0:1],
            lnr[:, p2:p2 + 1], ALU.subtract, ALU.mult), mark=pV)
        ln_state["m_aff2"][p2] = m_aff
        ln_state["m_aff4"][p4] = m_aff
        kb.wait("gpsimd", pV, m_aff)
        t_w = kb.dma("gpsimd", dst, xnb[:, p2, :], dG)
        ln_state["xn_dma"][p2] = t_w
        if kind == "g":
            xn_tile_w[idx] = t_w
        else:
            xnd_tile_w[idx] = t_w

    ln_ld = [0]
    ln_st = [0]
    ln_done = [0]
    NJOBS = len(ln_jobs)

    def pump_ln(k):
        while k > 0 and ln_done[0] < NJOBS:
            if ln_ld[0] < NJOBS and ln_ld[0] - ln_done[0] <= 2:
                ln_load(ln_ld[0]); ln_ld[0] += 2
            if ln_st[0] < ln_ld[0] and (ln_ld[0] - ln_st[0] > 1
                                        or ln_ld[0] == NJOBS):
                ln_stats(ln_st[0]); ln_st[0] += 1
            if ln_done[0] < ln_st[0] and (ln_st[0] - ln_done[0] > 0
                                          and (ln_ld[0] - ln_done[0] > 2
                                               or ln_ld[0] == NJOBS)):
                ln_fin(ln_done[0]); ln_done[0] += 1
            k -= 1

    kb.wait("scalar", pV, m_eps)
    pump_ln(10)

    stage = p3.enter_context(nc.sbuf_tensor("stage", [128, 64, 128], BF16))
    xnT = p3.enter_context(nc.sbuf_tensor("xnT", [128, 2, 16, 1024], BF16))
    qth = p3.enter_context(nc.sbuf_tensor("qth", [128, S], BF16))
    kth = p3.enter_context(nc.sbuf_tensor("kth", [128, S], BF16))
    vtih = p3.enter_context(nc.sbuf_tensor("vtih", [128, S], BF16))
    vth = p3.enter_context(nc.sbuf_tensor("vth", [128, 64, 132], BF16))
    Ebuf = p3.enter_context(nc.sbuf_tensor("Ebuf", [128, 2, 1024], BF16))
    sqs = Ebuf.rearrange("p a b -> p (a b)")
    rq = p3.enter_context(nc.sbuf_tensor("rq", [128, 1], F32))
    rk = p3.enter_context(nc.sbuf_tensor("rk", [128, 1], F32))
    rstat = p3.enter_context(nc.sbuf_tensor("rstat", [128, 8], F32))
    negc = p3.enter_context(nc.sbuf_tensor("negc", [128, 1], F32))
    rlq = p3.enter_context(nc.sbuf_tensor("rlq", [128, 4], F32))

    kb.op("vector", lambda e: e.memset(vth[:, :, 128:132], 0.0))
    m_vones = kb.op("vector", lambda e: e.memset(vth[:, :, 128:129], 1.0), mark=pV)

    xnT_rd = [0, 0]
    stage_dma = 0
    st_wr = 0
    ops_free = 0
    exp_done = [0, 0]
    pv_done_for_E = [0, 0]

    for b in range(B):
        for hl in range(2):
            with ExitStack() as pq:
                qkvps = pq.enter_context(
                    nc.psum_tensor(f"qkvps{b}_{hl}", [128, 2048], F32))
                kb.wait("tensor", pV, ops_free)
                ev_done = [0, 0]
                for m in range(8):
                    par = m % 2
                    if b == 0:
                        pump_ln(8 if hl == 0 else 3)
                    need = xn_tile_w[b * 64 + m * 8 + 7]
                    assert need > 0, f"LN tile for mega b{b} m{m} not scheduled"
                    kb.wait("sync", pT, xnT_rd[par])
                    kb.wait("sync", dG, need)
                    r0 = b * S + m * 1024
                    for ic in range(16):
                        t_ld = kb.dma(
                            "sync", xnT[:, par, ic, :],
                            xn_full[r0:r0 + 1024, ic * 128:(ic + 1) * 128],
                            dS, transpose=True)
                    if b == 0 and hl == 0 and m == 0:
                        kb.wait("tensor", dG, t_wq)
                    kb.wait("tensor", dS, t_ld)
                    for j, oc in enumerate((hl, 2 + hl, 4 + hl)):
                        pp = j % 2
                        kb.wait("tensor", pV, ev_done[pp])
                        for ic in range(16):
                            for tb in range(2):
                                fn = lambda e, ic=ic, tb=tb, par=par, pp=pp, oc=oc: \
                                    e.matmul(
                                        qkvps[:, pp * 1024 + tb * 512:
                                              pp * 1024 + (tb + 1) * 512],
                                        wqkvT[:, ic, oc * 128:(oc + 1) * 128],
                                        xnT[:, par, ic, tb * 512:(tb + 1) * 512],
                                        start=(ic == 0), stop=(ic == 15))
                                if ic == 15 and tb == 1:
                                    m_mm = kb.op("tensor", fn, mark=pT)
                                else:
                                    kb.op("tensor", fn)
                        if j == 2:
                            xnT_rd[par] = m_mm
                        dst = (qth, kth, vtih)[j]
                        kb.wait("vector", pT, m_mm)
                        ev_done[pp] = kb.op(
                            "vector",
                            lambda e, dst=dst, m=m, pp=pp: e.tensor_copy(
                                dst[:, m * 1024:(m + 1) * 1024],
                                qkvps[:, pp * 1024:(pp + 1) * 1024]),
                            mark=pV)
                m_qkv_ev = pV["n"]

            with ExitStack() as pv_:
                tpsV = pv_.enter_context(
                    nc.psum_tensor(f"tpsV{b}_{hl}", [128, 2, 1024], BF16))
                kb.wait("tensor", pV, m_qkv_ev)
                if b == 0 and hl == 0:
                    kb.wait("tensor", pV, m_vones)
                tr_ev = [0, 0]
                for grp in range(16):
                    gp = grp % 2
                    kb.wait("tensor", pV, tr_ev[gp])
                    for i in range(4):
                        fn = lambda e, grp=grp, i=i, gp=gp: e.transpose(
                            tpsV[:, gp, i * 128:(i + 1) * 128],
                            vtih[:, (grp * 4 + i) * 128:(grp * 4 + i + 1) * 128],
                            idb[:])
                        if i == 3:
                            m_tr = kb.op("tensor", fn, mark=pT)
                        else:
                            kb.op("tensor", fn)
                    kb.wait("vector", pT, m_tr)
                    tr_ev[gp] = kb.op(
                        "vector",
                        lambda e, grp=grp, gp=gp: e.tensor_copy(
                            vth[:, grp * 4:(grp + 1) * 4, 0:128],
                            tpsV[:, gp, 0:512].rearrange("p (k d) -> p k d", k=4)),
                        mark=pV)
                m_vt = pV["n"]

            with ExitStack() as pa:
                sps = pa.enter_context(
                    nc.psum_tensor(f"sps{b}_{hl}", [128, 2, 1024], F32))
                ops = [pa.enter_context(
                    nc.psum_tensor(f"ops{b}_{hl}_{i}", [128, 512], F32))
                    for i in range(4)]

                kb.wait("tensor", pV, m_vt)
                kb.wait("scalar", pV, m_qkv_ev)
                kb.wait("scalar", pT, max(pv_done_for_E))
                m_red = 0
                spsf = sps.rearrange("p a b -> p (a b)")
                for which, src in enumerate((qth, kth)):
                    for chn in range(4):
                        m_sq2 = kb.op("scalar", lambda e, src=src, chn=chn: e.activation(
                            sqs[:], src[:, chn * 2048:(chn + 1) * 2048],
                            AF.Square), mark=pC)
                        kb.wait("tensor", pC, m_sq2)
                        kb.wait("tensor", pV, m_red)
                        for c4 in range(4):
                            fn = lambda e, c4=c4: e.matmul(
                                spsf[:, c4 * 512:(c4 + 1) * 512], onesb[:],
                                sqs[:, c4 * 512:(c4 + 1) * 512],
                                start=True, stop=True)
                            if c4 == 3:
                                m_nq = kb.op("tensor", fn, mark=pT)
                            else:
                                kb.op("tensor", fn)
                        kb.wait("vector", pT, m_nq)
                        idx = which * 4 + chn
                        m_red = kb.op("vector", lambda e, idx=idx: e.reduce_max(
                            rstat[:, idx:idx + 1], spsf[:], axis=AX.X), mark=pV)
                        kb.wait("scalar", pT, m_nq)
                kb.wait("vector", pV, m_red)
                kb.op("vector", lambda e: e.reduce_max(rq[:], rstat[:, 0:4], axis=AX.X))
                m_rk = kb.op("vector", lambda e: e.reduce_max(
                    rk[:], rstat[:, 4:8], axis=AX.X), mark=pV)
                kb.wait("vector", pV, m_rk)
                m_prod = kb.op("vector", lambda e: e.tensor_mul(rq[:], rq[:], rk[:]),
                               mark=pV)
                kb.wait("scalar", pV, m_prod)
                m_sqc = kb.op("scalar", lambda e: e.activation(
                    negc[:], rq[:], AF.Sqrt, scale=CMARG / 128.0), mark=pC)
                kb.wait("vector", pC, m_sqc)
                m_negc = kb.op("vector", lambda e: e.tensor_scalar_mul(
                    negc[:], negc[:], -1.0), mark=pV)
                kb.wait("scalar", pV, m_negc)

                kb.wait("tensor", pV, m_red)
                pend = [None]
                pv_last = {}
                norm_done = {}

                def emit_norm(qb2, hl=hl):
                    kb.wait("vector", pT, pv_last[qb2])
                    if qb2 == 0:
                        kb.wait("vector", dC, stage_dma)
                    for sub in range(3):
                        kb.op("vector", lambda e, sub=sub: e.reciprocal(
                            rlq[:, sub:sub + 1], ops[sub][:, 128:129]))
                    m_rl = kb.op("vector", lambda e: e.reciprocal(
                        rlq[:, 3:4], ops[3][:, 128:129]), mark=pV)
                    kb.wait("vector", pV, m_rl)
                    m_n = None
                    for sub in range(4):
                        fn = lambda e, sub=sub, qb2=qb2: e.tensor_scalar(
                            stage[:, qb2 * 4 + sub, :],
                            ops[sub][:, 0:128], rlq[:, sub:sub + 1], None, ALU.mult)
                        if sub == 3:
                            m_n = kb.op("vector", fn, mark=pV)
                        else:
                            kb.op("vector", fn)
                    norm_done[qb2] = m_n
                    return m_n

                for qb in range(16):
                    if b == 0:
                        pump_ln(2)
                    ng2 = 2 * (qb + 1)

                    def emit_S(g2, qb=qb):
                        epar = g2 % 2
                        kb.wait("tensor", pC, exp_done[epar])
                        for i in range(2):
                            kc = 2 * g2 + i
                            fn = lambda e, kc=kc, i=i, qb=qb, epar=epar: e.matmul(
                                sps[:, epar, i * 512:(i + 1) * 512],
                                kth[:, kc * 128:(kc + 1) * 128],
                                qth[:, qb * 512:(qb + 1) * 512],
                                start=True, stop=True)
                            if i == 1:
                                return kb.op("tensor", fn, mark=pT)
                            kb.op("tensor", fn)

                    def emit_exp(g2, m_s, qb=qb):
                        epar = g2 % 2
                        diag = (g2 >= 2 * (qb + 1) - 2)
                        kb.wait("scalar", pT, m_s)
                        kb.wait("scalar", pT, pv_done_for_E[epar])
                        m_exp = kb.op("scalar", lambda e, epar=epar: e.activation(
                            Ebuf[:, epar, :], sps[:, epar, :], AF.Exp,
                            bias=negc[:], scale=SCALE), mark=pC)
                        exp_done[epar] = m_exp
                        m_msk = 0
                        if diag:
                            kb.wait("gpsimd", pC, m_exp)
                            for i in range(2):
                                kc = 2 * g2 + i
                                ii = kc - 4 * qb
                                fn = lambda e, epar=epar, i=i, ii=ii: e.affine_select(
                                    Ebuf[:, epar, i * 512:(i + 1) * 512],
                                    Ebuf[:, epar, i * 512:(i + 1) * 512],
                                    pattern=[[1, 512]], compare_op=ALU.is_ge,
                                    fill=0.0, base=-128 * ii, channel_multiplier=-1)
                                if i == 1:
                                    m_msk = kb.op("gpsimd", fn, mark=pP)
                                else:
                                    kb.op("gpsimd", fn)
                        return m_exp, m_msk

                    def emit_PV(qb2, g2, m_exp, m_msk):
                        epar = g2 % 2
                        ng2i = 2 * (qb2 + 1)
                        kb.wait("tensor", pC, m_exp)
                        if m_msk:
                            kb.wait("tensor", pP, m_msk)
                        if g2 == 0:
                            kb.wait("tensor", pV, norm_done[qb2 - 1] if qb2 > 0
                                    else ops_free)
                        m_pv = None
                        for i in range(2):
                            for sub in range(4):
                                fn = lambda e, epar=epar, i=i, sub=sub, g2=g2,                                     ng2i=ng2i: e.matmul(
                                        ops[sub][:, 0:129],
                                        Ebuf[:, epar, i * 512 + sub * 128:
                                             i * 512 + (sub + 1) * 128],
                                        vth[:, 2 * g2 + i, 0:129],
                                        start=(g2 == 0 and i == 0),
                                        stop=(g2 == ng2i - 1 and i == 1))
                                if i == 1 and sub == 3:
                                    m_pv = kb.op("tensor", fn, mark=pT)
                                else:
                                    kb.op("tensor", fn)
                        pv_done_for_E[epar] = m_pv
                        pv_last[qb2] = m_pv
                        return m_pv

                    for g2 in range(ng2):
                        m_s = emit_S(g2)
                        if pend[0] is not None:
                            emit_PV(*pend[0])
                        pend[0] = (qb, g2) + emit_exp(g2, m_s)
                        if qb > 0 and g2 == 0 and (qb - 1) in pv_last                            and (qb - 1) not in norm_done:
                            emit_norm(qb - 1)

                if pend[0] is not None:
                    emit_PV(*pend[0])
                    pend[0] = None
                emit_norm(15)
                ops_free = norm_done[15]
                st_wr = norm_done[15]

            kb.wait("scalar", pV, st_wr)
            stage_dma = kb.dma(
                "scalar",
                at_d[b * S:(b + 1) * S, hl * 128:(hl + 1) * 128].rearrange(
                    "(p sub) f -> p sub f", p=128),
                stage[:], dC)

        kb.wait("gpsimd", dC, stage_dma)
        kb.op("gpsimd", lambda e, b=b: e.collective_compute(
            "AllGather", ALU.bypass, replica_groups=[list(range(W))],
            ins=[at_d[b * S:(b + 1) * S, :].opt()],
            outs=[at_all[b * W * S:(b + 1) * W * S, :].opt()]), mark=cc)

    assert ln_done[0] == NJOBS, f"LN jobs left: {NJOBS - ln_done[0]}"

    kb.op("gpsimd", lambda e: e.collective_compute(
        "AllReduce", ALU.add, replica_groups=[list(range(W))],
        ins=[fl_in.opt()], outs=[fl_out.opt()]), mark=cc)
    cc_at = cc["n"]

    kb.barrier(bar)
    p3.close()

    attnT = st.enter_context(nc.sbuf_tensor("attnT", [128, 16, 2048], BF16))
    gT = st.enter_context(nc.sbuf_tensor("gT", [128, 2, 2048], BF16))
    alpear = st.enter_context(nc.sbuf_tensor("alpear", [128, 512], F32))
    p5a = ExitStack()
    xnTo = p5a.enter_context(nc.sbuf_tensor("xnTo", [128, 16, 2048], BF16))
    awf = p5a.enter_context(nc.sbuf_tensor("awf", [128, 16, 512], BF16))

    kb.wait("sync", dG, max(xnd_tile_w))
    t_xnTo = 0
    for ic in range(16):
        for half in range(2):
            t_xnTo = kb.dma(
                "sync", xnTo[:, ic, half * 1024:(half + 1) * 1024],
                xn_d[half * 1024:(half + 1) * 1024, ic * 128:(ic + 1) * 128],
                dS, transpose=True)
    kb.wait("tensor", dS, t_xnTo)

    silu_done = 0
    gT_dma = [0, 0]
    m_bw = 0
    m_ffn = 0
    for qu in range(4):
        with ExitStack() as pw:
            wfc = pw.enter_context(nc.sbuf_tensor(f"wfc{qu}", [128, 16, 512], F32))
            if qu > 0:
                kb.wait("scalar", pT, m_ffn)
            t_wf = kb.dma("scalar", wfc[:],
                          wf_t[:, qu * 512:(qu + 1) * 512].rearrange(
                              "(ic p) o -> p ic o", p=128), dC)
            kb.wait("scalar", dC, t_wf)
            if qu > 0:
                kb.wait("scalar", pT, m_ffn)
            m_abs = None
            for ic in range(16):
                m_abs = kb.op("scalar", lambda e, ic=ic: e.activation(
                    awf[:, ic, :], wfc[:, ic, :], AF.Abs), mark=pC)
            with ExitStack() as palp:
                aps = palp.enter_context(nc.psum_tensor(f"aps{qu}", [128, 512], F32))
                kb.wait("tensor", pC, m_abs)
                if qu > 0:
                    kb.wait("tensor", pC, silu_done)
                for ic in range(16):
                    fn = lambda e, ic=ic: e.matmul(
                        aps[:], onesb[:], awf[:, ic, :],
                        start=(ic == 0), stop=(ic == 15))
                    if ic == 15:
                        m_aps = kb.op("tensor", fn, mark=pT)
                    else:
                        kb.op("tensor", fn)
                kb.wait("vector", pT, m_aps)
                if qu > 0:
                    kb.wait("vector", pV, m_bw)
                m_alp = kb.op("vector", lambda e: e.tensor_scalar_mul(
                    alpear[:], aps[:], 1.0 / 2048.0), mark=pV)
            kb.wait("vector", pV, m_alp)
            for ic in range(16):
                kb.op("vector", lambda e, ic=ic: e.tensor_single_scalar(
                    awf[:, ic, :], wfc[:, ic, :], 0.0, ALU.is_ge))
                kb.op("vector", lambda e, ic=ic: e.tensor_scalar(
                    awf[:, ic, :], awf[:, ic, :], 2.0, 1.0,
                    ALU.mult, ALU.subtract))
                m_bw = kb.op("vector", lambda e, ic=ic: e.tensor_mul(
                    awf[:, ic, :], awf[:, ic, :], alpear[:]), mark=pV)
        with ExitStack() as pps:
            gps = pps.enter_context(nc.psum_tensor(f"gps{qu}", [128, 2048], F32))
            for fl in range(4):
                fc = qu * 4 + fl
                gpar = fc % 2
                kb.wait("tensor", pC, silu_done)
                if fl == 0:
                    kb.wait("tensor", pV, m_bw)
                for ic in range(16):
                    for tb in range(4):
                        fn = lambda e, ic=ic, fl=fl, tb=tb: e.matmul(
                            gps[:, tb * 512:(tb + 1) * 512],
                            awf[:, ic, fl * 128:(fl + 1) * 128],
                            xnTo[:, ic, tb * 512:(tb + 1) * 512],
                            start=(ic == 0), stop=(ic == 15))
                        if ic == 15 and tb == 3:
                            m_ffn = kb.op("tensor", fn, mark=pT)
                        else:
                            kb.op("tensor", fn)
                kb.wait("scalar", pT, m_ffn)
                kb.wait("scalar", dS, gT_dma[gpar])
                silu_done = kb.op("scalar", lambda e, fc=fc, gpar=gpar: e.activation(
                    gT[:, gpar, :], gps[:], AF.Silu,
                    bias=fbs[:, fc:fc + 1], scale=1.0), mark=pC)
                kb.wait("sync", pC, silu_done)
                gT_dma[gpar] = kb.dma(
                    "sync", g_d[fc * 128:(fc + 1) * 128, :], gT[:, gpar, :], dS)
    p5a.close()

    with ExitStack() as pg:
        gbuf = pg.enter_context(nc.sbuf_tensor("gbuf", [128, 16, 256], BF16))
        tpsC = pg.enter_context(nc.psum_tensor("tpsC", [128, 2, 1024], BF16))
        kb.wait("gpsimd", cc, cc_at)
        kb.wait("gpsimd", dS, t_gix)
        kb.wait("gpsimd", pT, m_ffn)
        kb.wait("gpsimd", pC, silu_done)
        kb.wait("tensor", pC, silu_done)
        tr_rd = [0] * 16
        ps_rd = [0, 0]
        it = 0
        for j in range(16):
            for r in range(8):
                sl = it % 16
                par = it % 2
                kb.wait("gpsimd", pV, tr_rd[sl])
                t_g = kb.idma(gbuf[:, sl, :], at_all, gix[:, j, r:r + 1], dG)
                kb.wait("tensor", dG, t_g)
                kb.wait("tensor", pV, ps_rd[par])
                for u in range(2):
                    fn = lambda e, sl=sl, par=par, u=u: e.transpose(
                        tpsC[:, par, u * 128:(u + 1) * 128],
                        gbuf[:, sl, u * 128:(u + 1) * 128], idb[:])
                    if u == 1:
                        m_trc = kb.op("tensor", fn, mark=pT)
                    else:
                        kb.op("tensor", fn)
                kb.wait("vector", pT, m_trc)
                m_ev = kb.op("vector", lambda e, j=j, r=r, par=par: e.tensor_copy(
                    attnT[:, 2 * r:2 * r + 2, j * 128:(j + 1) * 128],
                    tpsC[:, par, 0:256].rearrange("p (k d) -> p k d", k=2)),
                    mark=pV)
                tr_rd[sl] = m_ev
                ps_rd[par] = m_ev
                it += 1
        m_attnT = pV["n"]

    xoT = st.enter_context(nc.sbuf_tensor("xoT", [128, 2, 2048], F32))
    ystg = st.enter_context(nc.sbuf_tensor("ystg", [128, 2, 2048], F32))
    gTi = st.enter_context(nc.sbuf_tensor("gTi", [128, 2, 2048], BF16))
    woqf = st.enter_context(nc.sbuf_tensor("woqf", [128, 16, 512], F32))
    woq = st.enter_context(nc.sbuf_tensor("woq", [128, 2, 16, 512], BF16))

    kb.wait("tensor", pV, m_attnT)
    kb.wait("sync", pV, m_attnT)
    kb.wait("scalar", pV, m_attnT)
    kb.wait("gpsimd", pV, m_attnT)

    y_dma = 0
    m_s2 = 0
    m_s2_q = {}
    cast_pP = {}
    t_wof = {}
    gTi_rd = [0, 0]
    y_dma_par = [0, 0]
    xo_rd = [(pV, 0), (pV, 0)]

    def emit_wload(qu):
        if qu >= 4:
            return
        if qu > 0:
            kb.wait("scalar", pP, cast_pP[qu - 1])
        t_wof[qu] = kb.dma("scalar", woqf[:],
                           wo_t[:, qu * 512:(qu + 1) * 512].rearrange(
                               "(ic p) o -> p ic o", p=128), dC)

    def emit_wcast(qu):
        if qu >= 4:
            return
        kb.wait("gpsimd", dC, t_wof[qu])
        if qu >= 2:
            kb.wait("gpsimd", pT, m_s2_q[qu - 2])
        m_c = None
        for ic in range(16):
            fn = lambda e, qu=qu, ic=ic: e.tensor_copy(
                woq[:, qu % 2, ic, :], woqf[:, ic, :])
            if ic == 15:
                m_c = kb.op("gpsimd", fn, mark=pP)
            else:
                kb.op("gpsimd", fn)
        cast_pP[qu] = m_c

    emit_wload(0)
    emit_wcast(0)
    emit_wload(1)
    with ExitStack() as pps:
        sps2ab = [pps.enter_context(
            nc.psum_tensor(f"sps2_{i}", [128, 2048], F32))
            for i in range(2)]
        for qu in range(4):
            for fl in range(4):
                fc = qu * 4 + fl
                par = fc % 2
                sps2 = sps2ab[par][:]
                kb.wait("sync", *xo_rd[par])
                xo_dma = kb.dma("sync", xoT[:, par, :],
                                x_own_t[fc * 128:(fc + 1) * 128, :], dS)
                kb.wait("sync", pV, gTi_rd[par])
                gt_dma = kb.dma("sync", gTi[:, par, :],
                                g_d[fc * 128:(fc + 1) * 128, :], dS)
                kb.wait("tensor", pV, gTi_rd[par])
                if fl == 0:
                    kb.wait("tensor", pP, cast_pP[qu])
                for ic in range(16):
                    for tb in range(4):
                        fn = lambda e, ic=ic, fl=fl, qu=qu, tb=tb, sps2=sps2: \
                            e.matmul(
                                sps2[:, tb * 512:(tb + 1) * 512],
                                woq[:, qu % 2, ic, fl * 128:(fl + 1) * 128],
                                attnT[:, ic, tb * 512:(tb + 1) * 512],
                                start=(ic == 0), stop=(ic == 15))
                        if ic == 15 and tb == 3:
                            m_s2 = kb.op("tensor", fn, mark=pT)
                        else:
                            kb.op("tensor", fn)
                if fl == 3:
                    m_s2_q[qu] = m_s2
                if fl == 0:
                    emit_wcast(qu + 1)
                    emit_wload(qu + 2)
                kb.wait("vector", pT, m_s2)
                kb.wait("vector", dS, gt_dma)
                kb.wait("vector", dS, y_dma_par[par])
                m_mult = kb.op(
                    "vector", lambda e, par=par, sps2=sps2: e.scalar_tensor_tensor(
                        ystg[:, par, :], sps2[:], 1.0, gTi[:, par, :],
                        ALU.bypass, ALU.mult), mark=pV)
                gTi_rd[par] = m_mult
                if fc % 2 == 0:
                    kb.wait("vector", dS, xo_dma)
                    m_add = kb.op("vector", lambda e, par=par: e.tensor_add(
                        ystg[:, par, :], ystg[:, par, :], xoT[:, par, :]),
                        mark=pV)
                    add_tok = (pV, m_add)
                else:
                    kb.wait("gpsimd", pV, m_mult)
                    kb.wait("gpsimd", dS, xo_dma)
                    m_add = kb.op("gpsimd", lambda e, par=par: e.tensor_add(
                        ystg[:, par, :], ystg[:, par, :], xoT[:, par, :]),
                        mark=pP)
                    add_tok = (pP, m_add)
                xo_rd[par] = add_tok
                kb.wait("sync", *add_tok)
                y_dma = kb.dma("sync", y_t[fc * 128:(fc + 1) * 128, :],
                               ystg[:, par, :], dS)
                y_dma_par[par] = y_dma

    if debug:
        o_attnT = nc.dram_tensor("o_attnT", [128, 16, 2048], BF16,
                                 kind="ExternalOutput").ap()
        o_atd = nc.dram_tensor("o_atd", [TOK, 256], BF16, kind="ExternalOutput").ap()
        o_gd = nc.dram_tensor("o_gd", [H, CH], BF16, kind="ExternalOutput").ap()
        o_xnd = nc.dram_tensor("o_xnd", [CH, H], BF16, kind="ExternalOutput").ap()
        o_xnf = nc.dram_tensor("o_xnf", [TOK, H], BF16, kind="ExternalOutput").ap()
        kb.wait("sync", dS, y_dma)
        kb.dma("sync", o_attnT, attnT[:], dS)
        kb.dma("sync", o_gd, g_d, dS)
        kb.dma("sync", o_xnd, xn_d, dS)
        kb.dma("sync", o_xnf, xn_full, dS)
        y_dma = kb.dma("sync", o_atd, at_d, dS)

    kb.wait("sync", dS, y_dma)

    with nc.Block() as blk:
        kb.emit(blk)
    st.close()
    return nc


_NC_CACHE = None


def _get_nc():
    global _NC_CACHE
    if _NC_CACHE is None:
        _NC_CACHE = build()
    return _NC_CACHE


def make_in_maps(L0_attn_hidden_states, L0_attn_w_qkv, L0_attn_w_o, L0_ffn_weight,
                 L0_ffn_bias):
    x = np.ascontiguousarray(np.asarray(L0_attn_hidden_states, dtype=np.float32)
                             ).reshape(TOK, H)
    wqkv = np.asarray(L0_attn_w_qkv, dtype=np.float32)
    wo = np.asarray(L0_attn_w_o, dtype=np.float32)
    wf = np.asarray(L0_ffn_weight, dtype=np.float32)
    fbv = np.asarray(L0_ffn_bias, dtype=np.float32)

    wo_t = np.ascontiguousarray(wo.T)
    wf_t = np.ascontiguousarray(wf.T)
    fb_l = np.ascontiguousarray(fbv.reshape(16, 128).T)

    in_maps = []
    for c in range(W):
        sl = np.concatenate([
            wqkv[256 * c:256 * c + 256],
            wqkv[H + 256 * c:H + 256 * c + 256],
            wqkv[2 * H + 256 * c:2 * H + 256 * c + 256],
        ], axis=0)
        wq_tc = np.ascontiguousarray(sl.T)
        x_own = np.ascontiguousarray(x[c * CH:(c + 1) * CH])
        x_own_t = np.ascontiguousarray(x_own.T)
        gi = np.zeros((128, 16, 8), dtype=np.int32)
        p = np.arange(128)
        for j in range(16):
            g = c * CH + 128 * j + p
            bb = g // S
            s = g % S
            sub = s // 128
            pp = s % 128
            row = pp * 64 + sub
            for r in range(8):
                gi[:, j, r] = bb * W * S + r * S + row
        in_maps.append({
            "x": x, "x_own": x_own, "x_own_t": x_own_t, "wq_t": wq_tc,
            "wo_t": wo_t, "wf_t": wf_t, "fb": fb_l, "gidx": gi,
        })
    return in_maps


def kernel(L0_attn_hidden_states, L0_attn_w_qkv, L0_attn_w_o, L0_ffn_weight,
           L0_ffn_bias):
    in_maps = make_in_maps(L0_attn_hidden_states, L0_attn_w_qkv, L0_attn_w_o,
                           L0_ffn_weight, L0_ffn_bias)
    nc = _get_nc()
    res = run_bass_kernel_spmd(nc, in_maps, core_ids=list(range(W)))
    y = np.concatenate([np.asarray(res.results[c]["y_t"]).T for c in range(W)],
                       axis=0)
    return np.ascontiguousarray(y.reshape(B, S, H))


# version 17
# speedup vs baseline: 1.0901x; 1.0901x over previous
from contextlib import ExitStack

import numpy as np
import concourse.bass as bass
import concourse.mybir as mybir
from concourse.bass_core_rust import IndirectOffsetOnAxis
from concourse.bass_utils import run_bass_kernel_spmd

F32 = mybir.dt.float32
BF16 = mybir.dt.bfloat16
I32 = mybir.dt.int32
AF = mybir.ActivationFunctionType
ALU = mybir.AluOpType
AX = mybir.AxisListType

W = 8
B = 2
S = 8192
H = 2048
TOK = B * S
CH = TOK // W
SCALE = 1.0 / float(np.sqrt(np.float32(128.0)))
CMARG = 1.0202

ENGINES = ("sync", "vector", "scalar", "tensor", "gpsimd")


class KB:

    def __init__(self, nc):
        self.nc = nc
        self.q = {e: [] for e in ENGINES}

    def mksem(self, stack, name):
        h = stack.enter_context(self.nc.semaphore(name))
        return {"h": h, "n": 0, "name": name}

    def on(self, eng, fn):
        self.q[eng].append(fn)

    def wait(self, eng, sem, n):
        if n <= 0:
            return
        h = sem["h"]
        self.on(eng, lambda e, h=h, n=n: e.wait_ge(h, n))

    def op(self, eng, fn, mark=None):
        if mark is None:
            self.on(eng, fn)
            return None
        mark["n"] += 1
        h, tgt = mark["h"], mark["n"]
        self.on(eng, lambda e, fn=fn, h=h: fn(e).then_inc(h, 1))
        return tgt

    def dma(self, eng, out, in_, sem, transpose=False):
        sem["n"] += 16
        h, tgt = sem["h"], sem["n"]

        def fn(e, out=out, in_=in_, h=h):
            if transpose:
                return e.dma_start_transpose(out, in_).then_inc(h, 16)
            return e.dma_start(out=out, in_=in_).then_inc(h, 16)

        self.on(eng, fn)
        return tgt

    def idma(self, out, in_, off_ap, sem):
        sem["n"] += 16
        h, tgt = sem["h"], sem["n"]
        self.on("gpsimd", lambda e, out=out, in_=in_, off=off_ap, h=h:
                e.indirect_dma_start(
                    out=out, out_offset=None, in_=in_,
                    in_offset=IndirectOffsetOnAxis(off, axis=0)).then_inc(h, 16))
        return tgt

    def barrier(self, sem):
        sem["n"] += len(ENGINES)
        tgt, h = sem["n"], sem["h"]
        for eng in ENGINES:
            self.on(eng, lambda e, h=h: e.nop().then_inc(h, 1))
            self.on(eng, lambda e, h=h, t=tgt: e.wait_ge(h, t))

    def emit(self, blk):
        for eng in ENGINES:
            fns = self.q[eng]

            def body(e, fns=fns):
                for f in fns:
                    f(e)

            getattr(blk, eng)(body)


def build(debug=False):
    nc = bass.Bass("TRN2", target_bir_lowering=False)

    x_in = nc.dram_tensor("x", [TOK, H], F32, kind="ExternalInput").ap()
    x_own = nc.dram_tensor("x_own", [CH, H], F32, kind="ExternalInput").ap()
    x_own_t = nc.dram_tensor("x_own_t", [H, CH], F32, kind="ExternalInput").ap()
    wq_t = nc.dram_tensor("wq_t", [H, 768], F32, kind="ExternalInput").ap()
    wo_t = nc.dram_tensor("wo_t", [H, H], F32, kind="ExternalInput").ap()
    wf_t = nc.dram_tensor("wf_t", [H, H], F32, kind="ExternalInput").ap()
    fb_in = nc.dram_tensor("fb", [128, 16], F32, kind="ExternalInput").ap()
    gidx = nc.dram_tensor("gidx", [128, 16, 8], I32, kind="ExternalInput").ap()
    y_t = nc.dram_tensor("y_t", [H, CH], F32, kind="ExternalOutput").ap()

    xn_d = nc.dram_tensor("xn_d", [CH, H], BF16, kind="Internal").ap()
    xn_full = nc.dram_tensor("xn_full", [TOK, H], BF16, kind="Internal").ap()
    g_d = nc.dram_tensor("g_d", [H, CH], BF16, kind="Internal").ap()
    at_d = nc.dram_tensor("at_d", [TOK, 256], BF16, kind="Internal").ap()
    at_all = nc.dram_tensor("at_all", [2 * W * S, 256], BF16, kind="Internal").ap()
    fl_in = nc.dram_tensor("fl_in", [128, 4], I32, kind="Internal").ap()
    fl_out = nc.dram_tensor("fl_out", [128, 4], I32, kind="Internal").ap()

    kb = KB(nc)
    st = ExitStack()

    dS = kb.mksem(st, "dS")
    dC = kb.mksem(st, "dC")
    dG = kb.mksem(st, "dG")
    cc = kb.mksem(st, "cc")
    pT = kb.mksem(st, "pT")
    pV = kb.mksem(st, "pV")
    pC = kb.mksem(st, "pC")
    pP = kb.mksem(st, "pP")
    bar = kb.mksem(st, "bar")

    idb = st.enter_context(nc.sbuf_tensor("idb", [128, 128], BF16))
    onesb = st.enter_context(nc.sbuf_tensor("onesb", [128, 128], BF16))
    idio = st.enter_context(nc.sbuf_tensor("idio", [128, 128], F32))
    epsb = st.enter_context(nc.sbuf_tensor("epsb", [128, 1], F32))
    gix = st.enter_context(nc.sbuf_tensor("gix", [128, 16, 8], I32))
    fbs = st.enter_context(nc.sbuf_tensor("fbs", [128, 16], F32))

    t_gix = kb.dma("sync", gix[:], gidx, dS)
    kb.dma("sync", fbs[:], fb_in, dS)
    kb.op("gpsimd", lambda e: e.iota(
        idio[:], pattern=[[1, 128]], base=0, channel_multiplier=-1,
        allow_small_or_imprecise_dtypes=True), mark=pP)
    kb.wait("vector", pP, pP["n"])
    kb.op("vector", lambda e: e.tensor_single_scalar(idb[:], idio[:], 0.0, ALU.is_equal))
    kb.op("vector", lambda e: e.memset(onesb[:], 1.0))
    m_eps = kb.op("vector", lambda e: e.memset(epsb[:], 1e-5), mark=pV)
    kb.wait("sync", pV, m_eps)
    kb.dma("sync", fl_in, gix.rearrange("p a b -> p (a b)")[:, 0:4], dS)

    p3 = ExitStack()
    wqkvT = p3.enter_context(nc.sbuf_tensor("wqkvT", [128, 16, 768], BF16))
    t_wq = kb.dma("gpsimd", wqkvT[:],
                  wq_t.rearrange("(ic p) o -> p ic o", p=128), dG)

    lnbufs = p3.enter_context(nc.sbuf_tensor("xof", [128, 4, H], BF16))
    xnb = p3.enter_context(nc.sbuf_tensor("xnb", [128, 2, H], BF16))
    bnst = p3.enter_context(nc.sbuf_tensor("bnst", [128, 24], F32))
    bnagp = p3.enter_context(nc.sbuf_tensor("bnagp", [128, 2, 2], F32))
    lnstd = p3.enter_context(nc.sbuf_tensor("lnstd", [128, 2], F32))
    lnr = p3.enter_context(nc.sbuf_tensor("lnr", [128, 2], F32))

    ln_state = {"m_sq": [0, 0], "m_aff2": [0, 0], "m_aff4": [0, 0, 0, 0],
                "xn_dma": [0, 0], "t_x": [0, 0, 0, 0]}
    xn_tile_w = [0] * 128
    xnd_tile_w = [0] * 16

    ln_jobs = ([("g", g) for g in range(64)] + [("o", t) for t in range(16)]
               + [("g", g) for g in range(64, 128)])

    def ln_load(t):
        kind, idx = ln_jobs[t]
        p4 = t % 4
        src = (x_in if kind == "g" else x_own)[idx * 128:(idx + 2) * 128, :]
        kb.wait("gpsimd", pV, ln_state["m_aff4"][p4])
        kb.wait("gpsimd", pV, ln_state["m_aff4"][p4 + 1])
        tx = kb.dma("gpsimd", lnbufs[:, p4:p4 + 2, :],
                    src.rearrange("(tt p) n -> p tt n", p=128), dG)
        ln_state["t_x"][p4] = tx
        ln_state["t_x"][p4 + 1] = tx

    def ln_stats(t):
        p4, p2 = t % 4, t % 2
        kb.wait("vector", dG, ln_state["t_x"][p4])
        kb.wait("vector", pV, ln_state["m_aff2"][p2])
        for c4 in range(3):
            kb.op("vector", lambda e, p4=p4, c4=c4: e.bn_stats(
                bnst[:, c4 * 6:(c4 + 1) * 6],
                lnbufs[:, p4, c4 * 512:(c4 + 1) * 512]))
        m_st = kb.op("vector", lambda e, p4=p4: e.bn_stats(
            bnst[:, 18:24], lnbufs[:, p4, 1536:2048]), mark=pV)
        kb.wait("vector", pV, m_st)
        m_bn = kb.op("vector", lambda e, p2=p2: e.bn_aggr(
            bnagp[:, p2, :], bnst[:]), mark=pV)
        kb.wait("scalar", pV, m_bn)
        ln_state["m_sq"][p2] = kb.op("scalar", lambda e, p2=p2: e.activation(
            lnstd[:, p2:p2 + 1], bnagp[:, p2, 1:2], AF.Sqrt,
            bias=epsb[:], scale=1.0), mark=pC)

    def ln_fin(t):
        kind, idx = ln_jobs[t]
        p4, p2 = t % 4, t % 2
        dst = (xn_full if kind == "g" else xn_d)[idx * 128:(idx + 1) * 128, :]
        kb.wait("vector", pC, ln_state["m_sq"][p2])
        m_rc = kb.op("vector", lambda e, p2=p2: e.reciprocal(
            lnr[:, p2:p2 + 1], lnstd[:, p2:p2 + 1]), mark=pV)
        kb.wait("vector", pV, m_rc)
        kb.wait("vector", dG, ln_state["xn_dma"][p2])
        m_aff = kb.op("vector", lambda e, p4=p4, p2=p2: e.tensor_scalar(
            xnb[:, p2, :], lnbufs[:, p4, :], bnagp[:, p2, 0:1],
            lnr[:, p2:p2 + 1], ALU.subtract, ALU.mult), mark=pV)
        ln_state["m_aff2"][p2] = m_aff
        ln_state["m_aff4"][p4] = m_aff
        kb.wait("gpsimd", pV, m_aff)
        t_w = kb.dma("gpsimd", dst, xnb[:, p2, :], dG)
        ln_state["xn_dma"][p2] = t_w
        if kind == "g":
            xn_tile_w[idx] = t_w
        else:
            xnd_tile_w[idx] = t_w

    ln_ld = [0]
    ln_st = [0]
    ln_done = [0]
    NJOBS = len(ln_jobs)

    def pump_ln(k):
        while k > 0 and ln_done[0] < NJOBS:
            if ln_ld[0] < NJOBS and ln_ld[0] - ln_done[0] <= 2:
                ln_load(ln_ld[0]); ln_ld[0] += 2
            if ln_st[0] < ln_ld[0] and (ln_ld[0] - ln_st[0] > 1
                                        or ln_ld[0] == NJOBS):
                ln_stats(ln_st[0]); ln_st[0] += 1
            if ln_done[0] < ln_st[0] and (ln_st[0] - ln_done[0] > 0
                                          and (ln_ld[0] - ln_done[0] > 2
                                               or ln_ld[0] == NJOBS)):
                ln_fin(ln_done[0]); ln_done[0] += 1
            k -= 1

    kb.wait("scalar", pV, m_eps)
    pump_ln(10)

    stage = p3.enter_context(nc.sbuf_tensor("stage", [128, 64, 128], BF16))
    xnT = p3.enter_context(nc.sbuf_tensor("xnT", [128, 2, 16, 1024], BF16))
    qth = p3.enter_context(nc.sbuf_tensor("qth", [128, S], BF16))
    kth = p3.enter_context(nc.sbuf_tensor("kth", [128, S], BF16))
    vtih = p3.enter_context(nc.sbuf_tensor("vtih", [128, S], BF16))
    vth = p3.enter_context(nc.sbuf_tensor("vth", [128, 64, 132], BF16))
    Ebuf = p3.enter_context(nc.sbuf_tensor("Ebuf", [128, 2, 1024], BF16))
    sqs = Ebuf.rearrange("p a b -> p (a b)")
    rq = p3.enter_context(nc.sbuf_tensor("rq", [128, 1], F32))
    rk = p3.enter_context(nc.sbuf_tensor("rk", [128, 1], F32))
    rstat = p3.enter_context(nc.sbuf_tensor("rstat", [128, 8], F32))
    negc = p3.enter_context(nc.sbuf_tensor("negc", [128, 1], F32))
    rlq = p3.enter_context(nc.sbuf_tensor("rlq", [128, 4], F32))

    kb.op("vector", lambda e: e.memset(vth[:, :, 128:132], 0.0))
    m_vones = kb.op("vector", lambda e: e.memset(vth[:, :, 128:129], 1.0), mark=pV)

    xnT_rd = [0, 0]
    stage_dma = 0
    st_wr = 0
    ops_free = 0
    exp_done = [0, 0]
    pv_done_for_E = [0, 0]

    for b in range(B):
        for hl in range(2):
            with ExitStack() as pq:
                qkvps = pq.enter_context(
                    nc.psum_tensor(f"qkvps{b}_{hl}", [128, 2048], F32))
                kb.wait("tensor", pV, ops_free)
                ev_done = [0, 0]
                for m in range(8):
                    par = m % 2
                    if b == 0:
                        pump_ln(8 if hl == 0 else 3)
                    need = xn_tile_w[b * 64 + m * 8 + 7]
                    assert need > 0, f"LN tile for mega b{b} m{m} not scheduled"
                    kb.wait("sync", pT, xnT_rd[par])
                    kb.wait("sync", dG, need)
                    r0 = b * S + m * 1024
                    for ic in range(16):
                        t_ld = kb.dma(
                            "sync", xnT[:, par, ic, :],
                            xn_full[r0:r0 + 1024, ic * 128:(ic + 1) * 128],
                            dS, transpose=True)
                    if b == 0 and hl == 0 and m == 0:
                        kb.wait("tensor", dG, t_wq)
                    kb.wait("tensor", dS, t_ld)
                    for j, oc in enumerate((hl, 2 + hl, 4 + hl)):
                        pp = j % 2
                        kb.wait("tensor", pV, ev_done[pp])
                        for ic in range(16):
                            for tb in range(2):
                                fn = lambda e, ic=ic, tb=tb, par=par, pp=pp, oc=oc: \
                                    e.matmul(
                                        qkvps[:, pp * 1024 + tb * 512:
                                              pp * 1024 + (tb + 1) * 512],
                                        wqkvT[:, ic, oc * 128:(oc + 1) * 128],
                                        xnT[:, par, ic, tb * 512:(tb + 1) * 512],
                                        start=(ic == 0), stop=(ic == 15))
                                if ic == 15 and tb == 1:
                                    m_mm = kb.op("tensor", fn, mark=pT)
                                else:
                                    kb.op("tensor", fn)
                        if j == 2:
                            xnT_rd[par] = m_mm
                        dst = (qth, kth, vtih)[j]
                        kb.wait("vector", pT, m_mm)
                        ev_done[pp] = kb.op(
                            "vector",
                            lambda e, dst=dst, m=m, pp=pp: e.tensor_copy(
                                dst[:, m * 1024:(m + 1) * 1024],
                                qkvps[:, pp * 1024:(pp + 1) * 1024]),
                            mark=pV)
                m_qkv_ev = pV["n"]

            with ExitStack() as pv_:
                tpsV = pv_.enter_context(
                    nc.psum_tensor(f"tpsV{b}_{hl}", [128, 2, 1024], BF16))
                kb.wait("tensor", pV, m_qkv_ev)
                if b == 0 and hl == 0:
                    kb.wait("tensor", pV, m_vones)
                tr_ev = [0, 0]
                for grp in range(16):
                    gp = grp % 2
                    kb.wait("tensor", pV, tr_ev[gp])
                    for i in range(4):
                        fn = lambda e, grp=grp, i=i, gp=gp: e.transpose(
                            tpsV[:, gp, i * 128:(i + 1) * 128],
                            vtih[:, (grp * 4 + i) * 128:(grp * 4 + i + 1) * 128],
                            idb[:])
                        if i == 3:
                            m_tr = kb.op("tensor", fn, mark=pT)
                        else:
                            kb.op("tensor", fn)
                    kb.wait("vector", pT, m_tr)
                    tr_ev[gp] = kb.op(
                        "vector",
                        lambda e, grp=grp, gp=gp: e.tensor_copy(
                            vth[:, grp * 4:(grp + 1) * 4, 0:128],
                            tpsV[:, gp, 0:512].rearrange("p (k d) -> p k d", k=4)),
                        mark=pV)
                m_vt = pV["n"]

            with ExitStack() as pa:
                sps = pa.enter_context(
                    nc.psum_tensor(f"sps{b}_{hl}", [128, 2, 1024], F32))
                ops = [pa.enter_context(
                    nc.psum_tensor(f"ops{b}_{hl}_{i}", [128, 512], F32))
                    for i in range(4)]

                kb.wait("tensor", pV, m_vt)
                kb.wait("scalar", pV, m_qkv_ev)
                kb.wait("scalar", pT, max(pv_done_for_E))
                m_red = 0
                spsf = sps.rearrange("p a b -> p (a b)")
                for which, src in enumerate((qth, kth)):
                    for chn in range(4):
                        m_sq2 = kb.op("scalar", lambda e, src=src, chn=chn: e.activation(
                            sqs[:], src[:, chn * 2048:(chn + 1) * 2048],
                            AF.Square), mark=pC)
                        kb.wait("tensor", pC, m_sq2)
                        kb.wait("tensor", pV, m_red)
                        for c4 in range(4):
                            fn = lambda e, c4=c4: e.matmul(
                                spsf[:, c4 * 512:(c4 + 1) * 512], onesb[:],
                                sqs[:, c4 * 512:(c4 + 1) * 512],
                                start=True, stop=True)
                            if c4 == 3:
                                m_nq = kb.op("tensor", fn, mark=pT)
                            else:
                                kb.op("tensor", fn)
                        kb.wait("vector", pT, m_nq)
                        idx = which * 4 + chn
                        m_red = kb.op("vector", lambda e, idx=idx: e.reduce_max(
                            rstat[:, idx:idx + 1], spsf[:], axis=AX.X), mark=pV)
                        kb.wait("scalar", pT, m_nq)
                kb.wait("vector", pV, m_red)
                kb.op("vector", lambda e: e.reduce_max(rq[:], rstat[:, 0:4], axis=AX.X))
                m_rk = kb.op("vector", lambda e: e.reduce_max(
                    rk[:], rstat[:, 4:8], axis=AX.X), mark=pV)
                kb.wait("vector", pV, m_rk)
                m_prod = kb.op("vector", lambda e: e.tensor_mul(rq[:], rq[:], rk[:]),
                               mark=pV)
                kb.wait("scalar", pV, m_prod)
                m_sqc = kb.op("scalar", lambda e: e.activation(
                    negc[:], rq[:], AF.Sqrt, scale=CMARG / 128.0), mark=pC)
                kb.wait("vector", pC, m_sqc)
                m_negc = kb.op("vector", lambda e: e.tensor_scalar_mul(
                    negc[:], negc[:], -1.0), mark=pV)
                kb.wait("scalar", pV, m_negc)

                kb.wait("tensor", pV, m_red)
                pend = [None]
                pv_last = {}
                norm_done = {}

                def emit_norm(qb2, hl=hl):
                    kb.wait("vector", pT, pv_last[qb2])
                    if qb2 == 0:
                        kb.wait("vector", dC, stage_dma)
                    for sub in range(3):
                        kb.op("vector", lambda e, sub=sub: e.reciprocal(
                            rlq[:, sub:sub + 1], ops[sub][:, 128:129]))
                    m_rl = kb.op("vector", lambda e: e.reciprocal(
                        rlq[:, 3:4], ops[3][:, 128:129]), mark=pV)
                    kb.wait("vector", pV, m_rl)
                    m_n = None
                    for sub in range(4):
                        fn = lambda e, sub=sub, qb2=qb2: e.tensor_scalar(
                            stage[:, qb2 * 4 + sub, :],
                            ops[sub][:, 0:128], rlq[:, sub:sub + 1], None, ALU.mult)
                        if sub == 3:
                            m_n = kb.op("vector", fn, mark=pV)
                        else:
                            kb.op("vector", fn)
                    norm_done[qb2] = m_n
                    return m_n

                for qb in range(16):
                    if b == 0:
                        pump_ln(2)
                    ng2 = 2 * (qb + 1)

                    def emit_S(g2, qb=qb):
                        epar = g2 % 2
                        kb.wait("tensor", pC, exp_done[epar])
                        for i in range(2):
                            kc = 2 * g2 + i
                            fn = lambda e, kc=kc, i=i, qb=qb, epar=epar: e.matmul(
                                sps[:, epar, i * 512:(i + 1) * 512],
                                kth[:, kc * 128:(kc + 1) * 128],
                                qth[:, qb * 512:(qb + 1) * 512],
                                start=True, stop=True)
                            if i == 1:
                                return kb.op("tensor", fn, mark=pT)
                            kb.op("tensor", fn)

                    def emit_exp(g2, m_s, qb=qb):
                        epar = g2 % 2
                        diag = (g2 >= 2 * (qb + 1) - 2)
                        kb.wait("scalar", pT, m_s)
                        kb.wait("scalar", pT, pv_done_for_E[epar])
                        m_exp = kb.op("scalar", lambda e, epar=epar: e.activation(
                            Ebuf[:, epar, :], sps[:, epar, :], AF.Exp,
                            bias=negc[:], scale=SCALE), mark=pC)
                        exp_done[epar] = m_exp
                        m_msk = 0
                        if diag:
                            kb.wait("gpsimd", pC, m_exp)
                            for i in range(2):
                                kc = 2 * g2 + i
                                ii = kc - 4 * qb
                                fn = lambda e, epar=epar, i=i, ii=ii: e.affine_select(
                                    Ebuf[:, epar, i * 512:(i + 1) * 512],
                                    Ebuf[:, epar, i * 512:(i + 1) * 512],
                                    pattern=[[1, 512]], compare_op=ALU.is_ge,
                                    fill=0.0, base=-128 * ii, channel_multiplier=-1)
                                if i == 1:
                                    m_msk = kb.op("gpsimd", fn, mark=pP)
                                else:
                                    kb.op("gpsimd", fn)
                        return m_exp, m_msk

                    def emit_PV(qb2, g2, m_exp, m_msk):
                        epar = g2 % 2
                        ng2i = 2 * (qb2 + 1)
                        kb.wait("tensor", pC, m_exp)
                        if m_msk:
                            kb.wait("tensor", pP, m_msk)
                        if g2 == 0:
                            kb.wait("tensor", pV, norm_done[qb2 - 1] if qb2 > 0
                                    else ops_free)
                        m_pv = None
                        for i in range(2):
                            for sub in range(4):
                                fn = lambda e, epar=epar, i=i, sub=sub, g2=g2,                                     ng2i=ng2i: e.matmul(
                                        ops[sub][:, 0:129],
                                        Ebuf[:, epar, i * 512 + sub * 128:
                                             i * 512 + (sub + 1) * 128],
                                        vth[:, 2 * g2 + i, 0:129],
                                        start=(g2 == 0 and i == 0),
                                        stop=(g2 == ng2i - 1 and i == 1))
                                if i == 1 and sub == 3:
                                    m_pv = kb.op("tensor", fn, mark=pT)
                                else:
                                    kb.op("tensor", fn)
                        pv_done_for_E[epar] = m_pv
                        pv_last[qb2] = m_pv
                        return m_pv

                    for g2 in range(ng2):
                        m_s = emit_S(g2)
                        if pend[0] is not None:
                            emit_PV(*pend[0])
                        pend[0] = (qb, g2) + emit_exp(g2, m_s)
                        if qb > 0 and g2 == 0 and (qb - 1) in pv_last                            and (qb - 1) not in norm_done:
                            emit_norm(qb - 1)

                if pend[0] is not None:
                    emit_PV(*pend[0])
                    pend[0] = None
                emit_norm(15)
                ops_free = norm_done[15]
                st_wr = norm_done[15]

            kb.wait("scalar", pV, st_wr)
            stage_dma = kb.dma(
                "scalar",
                at_d[b * S:(b + 1) * S, hl * 128:(hl + 1) * 128].rearrange(
                    "(p sub) f -> p sub f", p=128),
                stage[:], dC)

        kb.wait("gpsimd", dC, stage_dma)
        kb.op("gpsimd", lambda e, b=b: e.collective_compute(
            "AllGather", ALU.bypass, replica_groups=[list(range(W))],
            ins=[at_d[b * S:(b + 1) * S, :].opt()],
            outs=[at_all[b * W * S:(b + 1) * W * S, :].opt()]), mark=cc)

    assert ln_done[0] == NJOBS, f"LN jobs left: {NJOBS - ln_done[0]}"

    kb.op("gpsimd", lambda e: e.collective_compute(
        "AllReduce", ALU.add, replica_groups=[list(range(W))],
        ins=[fl_in.opt()], outs=[fl_out.opt()]), mark=cc)
    cc_at = cc["n"]

    kb.barrier(bar)
    p3.close()

    attnT = st.enter_context(nc.sbuf_tensor("attnT", [128, 16, 2048], BF16))
    gT = st.enter_context(nc.sbuf_tensor("gT", [128, 2, 2048], BF16))
    alpear = st.enter_context(nc.sbuf_tensor("alpear", [128, 512], F32))
    p5a = ExitStack()
    xnTo = p5a.enter_context(nc.sbuf_tensor("xnTo", [128, 16, 2048], BF16))
    awf = p5a.enter_context(nc.sbuf_tensor("awf", [128, 16, 512], BF16))

    kb.wait("sync", dG, max(xnd_tile_w))
    t_xnTo = 0
    for ic in range(16):
        for half in range(2):
            t_xnTo = kb.dma(
                "sync", xnTo[:, ic, half * 1024:(half + 1) * 1024],
                xn_d[half * 1024:(half + 1) * 1024, ic * 128:(ic + 1) * 128],
                dS, transpose=True)
    kb.wait("tensor", dS, t_xnTo)

    silu_done = 0
    sil_ck = [0, 0, 0, 0]
    gT_dma = [0, 0]
    m_bw = 0
    m_ffn = 0
    for qu in range(4):
        with ExitStack() as pw:
            wfc = pw.enter_context(nc.sbuf_tensor(f"wfc{qu}", [128, 16, 512], F32))
            if qu > 0:
                kb.wait("scalar", pT, m_ffn)
            t_wf = kb.dma("scalar", wfc[:],
                          wf_t[:, qu * 512:(qu + 1) * 512].rearrange(
                              "(ic p) o -> p ic o", p=128), dC)
            kb.wait("scalar", dC, t_wf)
            if qu > 0:
                kb.wait("scalar", pT, m_ffn)
            m_abs = None
            for ic in range(16):
                m_abs = kb.op("scalar", lambda e, ic=ic: e.activation(
                    awf[:, ic, :], wfc[:, ic, :], AF.Abs), mark=pC)
            with ExitStack() as palp:
                aps = palp.enter_context(nc.psum_tensor(f"aps{qu}", [128, 512], F32))
                kb.wait("tensor", pC, m_abs)
                if qu > 0:
                    kb.wait("tensor", pC, silu_done)
                for ic in range(16):
                    fn = lambda e, ic=ic: e.matmul(
                        aps[:], onesb[:], awf[:, ic, :],
                        start=(ic == 0), stop=(ic == 15))
                    if ic == 15:
                        m_aps = kb.op("tensor", fn, mark=pT)
                    else:
                        kb.op("tensor", fn)
                kb.wait("vector", pT, m_aps)
                if qu > 0:
                    kb.wait("vector", pV, m_bw)
                m_alp = kb.op("vector", lambda e: e.tensor_scalar_mul(
                    alpear[:], aps[:], 1.0 / 2048.0), mark=pV)
            kb.wait("vector", pV, m_alp)
            for ic in range(16):
                kb.op("vector", lambda e, ic=ic: e.tensor_single_scalar(
                    awf[:, ic, :], wfc[:, ic, :], 0.0, ALU.is_ge))
                kb.op("vector", lambda e, ic=ic: e.tensor_scalar(
                    awf[:, ic, :], awf[:, ic, :], 2.0, 1.0,
                    ALU.mult, ALU.subtract))
                m_bw = kb.op("vector", lambda e, ic=ic: e.tensor_mul(
                    awf[:, ic, :], awf[:, ic, :], alpear[:]), mark=pV)
        with ExitStack() as pps:
            gps = pps.enter_context(nc.psum_tensor(f"gps{qu}", [128, 2048], F32))
            for fl in range(4):
                fc = qu * 4 + fl
                gpar = fc % 2
                if fl == 0:
                    kb.wait("tensor", pV, m_bw)
                for ic in range(16):
                    for tb in range(4):
                        if ic == 0:
                            kb.wait("tensor", pC, sil_ck[tb])
                        fn = lambda e, ic=ic, fl=fl, tb=tb: e.matmul(
                            gps[:, tb * 512:(tb + 1) * 512],
                            awf[:, ic, fl * 128:(fl + 1) * 128],
                            xnTo[:, ic, tb * 512:(tb + 1) * 512],
                            start=(ic == 0), stop=(ic == 15))
                        if ic == 15 and tb == 3:
                            m_ffn = kb.op("tensor", fn, mark=pT)
                        else:
                            kb.op("tensor", fn)
                kb.wait("scalar", pT, m_ffn)
                kb.wait("scalar", dS, gT_dma[gpar])
                for ck in range(4):
                    sil_ck[ck] = kb.op(
                        "scalar", lambda e, fc=fc, gpar=gpar, ck=ck: e.activation(
                            gT[:, gpar, ck * 512:(ck + 1) * 512],
                            gps[:, ck * 512:(ck + 1) * 512], AF.Silu,
                            bias=fbs[:, fc:fc + 1], scale=1.0), mark=pC)
                silu_done = sil_ck[3]
                kb.wait("sync", pC, silu_done)
                gT_dma[gpar] = kb.dma(
                    "sync", g_d[fc * 128:(fc + 1) * 128, :], gT[:, gpar, :], dS)
    p5a.close()

    with ExitStack() as pg:
        gbuf = pg.enter_context(nc.sbuf_tensor("gbuf", [128, 16, 256], BF16))
        tpsC = pg.enter_context(nc.psum_tensor("tpsC", [128, 2, 1024], BF16))
        kb.wait("gpsimd", cc, cc_at)
        kb.wait("gpsimd", dS, t_gix)
        kb.wait("gpsimd", pT, m_ffn)
        kb.wait("gpsimd", pC, silu_done)
        kb.wait("tensor", pC, silu_done)
        tr_rd = [0] * 16
        ps_rd = [0, 0]
        it = 0
        for j in range(16):
            for r in range(8):
                sl = it % 16
                par = it % 2
                kb.wait("gpsimd", pV, tr_rd[sl])
                t_g = kb.idma(gbuf[:, sl, :], at_all, gix[:, j, r:r + 1], dG)
                kb.wait("tensor", dG, t_g)
                kb.wait("tensor", pV, ps_rd[par])
                for u in range(2):
                    fn = lambda e, sl=sl, par=par, u=u: e.transpose(
                        tpsC[:, par, u * 128:(u + 1) * 128],
                        gbuf[:, sl, u * 128:(u + 1) * 128], idb[:])
                    if u == 1:
                        m_trc = kb.op("tensor", fn, mark=pT)
                    else:
                        kb.op("tensor", fn)
                kb.wait("vector", pT, m_trc)
                m_ev = kb.op("vector", lambda e, j=j, r=r, par=par: e.tensor_copy(
                    attnT[:, 2 * r:2 * r + 2, j * 128:(j + 1) * 128],
                    tpsC[:, par, 0:256].rearrange("p (k d) -> p k d", k=2)),
                    mark=pV)
                tr_rd[sl] = m_ev
                ps_rd[par] = m_ev
                it += 1
        m_attnT = pV["n"]

    xoT = st.enter_context(nc.sbuf_tensor("xoT", [128, 2, 2048], F32))
    ystg = st.enter_context(nc.sbuf_tensor("ystg", [128, 2, 2048], F32))
    gTi = st.enter_context(nc.sbuf_tensor("gTi", [128, 2, 2048], BF16))
    woqf = st.enter_context(nc.sbuf_tensor("woqf", [128, 16, 512], F32))
    woq = st.enter_context(nc.sbuf_tensor("woq", [128, 2, 16, 512], BF16))

    kb.wait("tensor", pV, m_attnT)
    kb.wait("sync", pV, m_attnT)
    kb.wait("scalar", pV, m_attnT)
    kb.wait("gpsimd", pV, m_attnT)

    y_dma = 0
    m_s2 = 0
    m_s2_q = {}
    cast_pP = {}
    t_wof = {}
    gTi_rd = [0, 0]
    y_dma_par = [0, 0]
    xo_rd = [(pV, 0), (pV, 0)]

    def emit_wload(qu):
        if qu >= 4:
            return
        if qu > 0:
            kb.wait("scalar", pP, cast_pP[qu - 1])
        t_wof[qu] = kb.dma("scalar", woqf[:],
                           wo_t[:, qu * 512:(qu + 1) * 512].rearrange(
                               "(ic p) o -> p ic o", p=128), dC)

    def emit_wcast(qu):
        if qu >= 4:
            return
        kb.wait("gpsimd", dC, t_wof[qu])
        if qu >= 2:
            kb.wait("gpsimd", pT, m_s2_q[qu - 2])
        m_c = None
        for ic in range(16):
            fn = lambda e, qu=qu, ic=ic: e.tensor_copy(
                woq[:, qu % 2, ic, :], woqf[:, ic, :])
            if ic == 15:
                m_c = kb.op("gpsimd", fn, mark=pP)
            else:
                kb.op("gpsimd", fn)
        cast_pP[qu] = m_c

    emit_wload(0)
    emit_wcast(0)
    emit_wload(1)
    with ExitStack() as pps:
        sps2ab = [pps.enter_context(
            nc.psum_tensor(f"sps2_{i}", [128, 2048], F32))
            for i in range(2)]
        for qu in range(4):
            for fl in range(4):
                fc = qu * 4 + fl
                par = fc % 2
                sps2 = sps2ab[par][:]
                kb.wait("sync", *xo_rd[par])
                xo_dma = kb.dma("sync", xoT[:, par, :],
                                x_own_t[fc * 128:(fc + 1) * 128, :], dS)
                kb.wait("sync", pV, gTi_rd[par])
                gt_dma = kb.dma("sync", gTi[:, par, :],
                                g_d[fc * 128:(fc + 1) * 128, :], dS)
                kb.wait("tensor", pV, gTi_rd[par])
                if fl == 0:
                    kb.wait("tensor", pP, cast_pP[qu])
                for ic in range(16):
                    for tb in range(4):
                        fn = lambda e, ic=ic, fl=fl, qu=qu, tb=tb, sps2=sps2: \
                            e.matmul(
                                sps2[:, tb * 512:(tb + 1) * 512],
                                woq[:, qu % 2, ic, fl * 128:(fl + 1) * 128],
                                attnT[:, ic, tb * 512:(tb + 1) * 512],
                                start=(ic == 0), stop=(ic == 15))
                        if ic == 15 and tb == 3:
                            m_s2 = kb.op("tensor", fn, mark=pT)
                        else:
                            kb.op("tensor", fn)
                if fl == 3:
                    m_s2_q[qu] = m_s2
                if fl == 0:
                    emit_wcast(qu + 1)
                    emit_wload(qu + 2)
                kb.wait("vector", pT, m_s2)
                kb.wait("vector", dS, gt_dma)
                kb.wait("vector", dS, y_dma_par[par])
                m_mult = kb.op(
                    "vector", lambda e, par=par, sps2=sps2: e.scalar_tensor_tensor(
                        ystg[:, par, :], sps2[:], 1.0, gTi[:, par, :],
                        ALU.bypass, ALU.mult), mark=pV)
                gTi_rd[par] = m_mult
                if fc % 2 == 0:
                    kb.wait("vector", dS, xo_dma)
                    m_add = kb.op("vector", lambda e, par=par: e.tensor_add(
                        ystg[:, par, :], ystg[:, par, :], xoT[:, par, :]),
                        mark=pV)
                    add_tok = (pV, m_add)
                else:
                    kb.wait("gpsimd", pV, m_mult)
                    kb.wait("gpsimd", dS, xo_dma)
                    m_add = kb.op("gpsimd", lambda e, par=par: e.tensor_add(
                        ystg[:, par, :], ystg[:, par, :], xoT[:, par, :]),
                        mark=pP)
                    add_tok = (pP, m_add)
                xo_rd[par] = add_tok
                kb.wait("sync", *add_tok)
                y_dma = kb.dma("sync", y_t[fc * 128:(fc + 1) * 128, :],
                               ystg[:, par, :], dS)
                y_dma_par[par] = y_dma

    if debug:
        o_attnT = nc.dram_tensor("o_attnT", [128, 16, 2048], BF16,
                                 kind="ExternalOutput").ap()
        o_atd = nc.dram_tensor("o_atd", [TOK, 256], BF16, kind="ExternalOutput").ap()
        o_gd = nc.dram_tensor("o_gd", [H, CH], BF16, kind="ExternalOutput").ap()
        o_xnd = nc.dram_tensor("o_xnd", [CH, H], BF16, kind="ExternalOutput").ap()
        o_xnf = nc.dram_tensor("o_xnf", [TOK, H], BF16, kind="ExternalOutput").ap()
        kb.wait("sync", dS, y_dma)
        kb.dma("sync", o_attnT, attnT[:], dS)
        kb.dma("sync", o_gd, g_d, dS)
        kb.dma("sync", o_xnd, xn_d, dS)
        kb.dma("sync", o_xnf, xn_full, dS)
        y_dma = kb.dma("sync", o_atd, at_d, dS)

    kb.wait("sync", dS, y_dma)

    with nc.Block() as blk:
        kb.emit(blk)
    st.close()
    return nc


_NC_CACHE = None


def _get_nc():
    global _NC_CACHE
    if _NC_CACHE is None:
        _NC_CACHE = build()
    return _NC_CACHE


def make_in_maps(L0_attn_hidden_states, L0_attn_w_qkv, L0_attn_w_o, L0_ffn_weight,
                 L0_ffn_bias):
    x = np.ascontiguousarray(np.asarray(L0_attn_hidden_states, dtype=np.float32)
                             ).reshape(TOK, H)
    wqkv = np.asarray(L0_attn_w_qkv, dtype=np.float32)
    wo = np.asarray(L0_attn_w_o, dtype=np.float32)
    wf = np.asarray(L0_ffn_weight, dtype=np.float32)
    fbv = np.asarray(L0_ffn_bias, dtype=np.float32)

    wo_t = np.ascontiguousarray(wo.T)
    wf_t = np.ascontiguousarray(wf.T)
    fb_l = np.ascontiguousarray(fbv.reshape(16, 128).T)

    in_maps = []
    for c in range(W):
        sl = np.concatenate([
            wqkv[256 * c:256 * c + 256],
            wqkv[H + 256 * c:H + 256 * c + 256],
            wqkv[2 * H + 256 * c:2 * H + 256 * c + 256],
        ], axis=0)
        wq_tc = np.ascontiguousarray(sl.T)
        x_own = np.ascontiguousarray(x[c * CH:(c + 1) * CH])
        x_own_t = np.ascontiguousarray(x_own.T)
        gi = np.zeros((128, 16, 8), dtype=np.int32)
        p = np.arange(128)
        for j in range(16):
            g = c * CH + 128 * j + p
            bb = g // S
            s = g % S
            sub = s // 128
            pp = s % 128
            row = pp * 64 + sub
            for r in range(8):
                gi[:, j, r] = bb * W * S + r * S + row
        in_maps.append({
            "x": x, "x_own": x_own, "x_own_t": x_own_t, "wq_t": wq_tc,
            "wo_t": wo_t, "wf_t": wf_t, "fb": fb_l, "gidx": gi,
        })
    return in_maps


def kernel(L0_attn_hidden_states, L0_attn_w_qkv, L0_attn_w_o, L0_ffn_weight,
           L0_ffn_bias):
    in_maps = make_in_maps(L0_attn_hidden_states, L0_attn_w_qkv, L0_attn_w_o,
                           L0_ffn_weight, L0_ffn_bias)
    nc = _get_nc()
    res = run_bass_kernel_spmd(nc, in_maps, core_ids=list(range(W)))
    y = np.concatenate([np.asarray(res.results[c]["y_t"]).T for c in range(W)],
                       axis=0)
    return np.ascontiguousarray(y.reshape(B, S, H))


# version 18
# speedup vs baseline: 1.1072x; 1.0157x over previous
from contextlib import ExitStack

import numpy as np
import concourse.bass as bass
import concourse.mybir as mybir
from concourse.bass_core_rust import IndirectOffsetOnAxis
from concourse.bass_utils import run_bass_kernel_spmd

F32 = mybir.dt.float32
BF16 = mybir.dt.bfloat16
I32 = mybir.dt.int32
AF = mybir.ActivationFunctionType
ALU = mybir.AluOpType
AX = mybir.AxisListType

W = 8
B = 2
S = 8192
H = 2048
TOK = B * S
CH = TOK // W
SCALE = 1.0 / float(np.sqrt(np.float32(128.0)))
CMARG = 1.0202

ENGINES = ("sync", "vector", "scalar", "tensor", "gpsimd")


class KB:

    def __init__(self, nc):
        self.nc = nc
        self.q = {e: [] for e in ENGINES}

    def mksem(self, stack, name):
        h = stack.enter_context(self.nc.semaphore(name))
        return {"h": h, "n": 0, "name": name}

    def on(self, eng, fn):
        self.q[eng].append(fn)

    def wait(self, eng, sem, n):
        if n <= 0:
            return
        h = sem["h"]
        self.on(eng, lambda e, h=h, n=n: e.wait_ge(h, n))

    def op(self, eng, fn, mark=None):
        if mark is None:
            self.on(eng, fn)
            return None
        mark["n"] += 1
        h, tgt = mark["h"], mark["n"]
        self.on(eng, lambda e, fn=fn, h=h: fn(e).then_inc(h, 1))
        return tgt

    def dma(self, eng, out, in_, sem, transpose=False):
        sem["n"] += 16
        h, tgt = sem["h"], sem["n"]

        def fn(e, out=out, in_=in_, h=h):
            if transpose:
                return e.dma_start_transpose(out, in_).then_inc(h, 16)
            return e.dma_start(out=out, in_=in_).then_inc(h, 16)

        self.on(eng, fn)
        return tgt

    def idma(self, out, in_, off_ap, sem):
        sem["n"] += 16
        h, tgt = sem["h"], sem["n"]
        self.on("gpsimd", lambda e, out=out, in_=in_, off=off_ap, h=h:
                e.indirect_dma_start(
                    out=out, out_offset=None, in_=in_,
                    in_offset=IndirectOffsetOnAxis(off, axis=0)).then_inc(h, 16))
        return tgt

    def barrier(self, sem):
        sem["n"] += len(ENGINES)
        tgt, h = sem["n"], sem["h"]
        for eng in ENGINES:
            self.on(eng, lambda e, h=h: e.nop().then_inc(h, 1))
            self.on(eng, lambda e, h=h, t=tgt: e.wait_ge(h, t))

    def emit(self, blk):
        for eng in ENGINES:
            fns = self.q[eng]

            def body(e, fns=fns):
                for f in fns:
                    f(e)

            getattr(blk, eng)(body)


def build(debug=False):
    nc = bass.Bass("TRN2", target_bir_lowering=False)

    x_in = nc.dram_tensor("x", [TOK, H], F32, kind="ExternalInput").ap()
    x_own = nc.dram_tensor("x_own", [CH, H], F32, kind="ExternalInput").ap()
    x_own_t = nc.dram_tensor("x_own_t", [H, CH], F32, kind="ExternalInput").ap()
    wq_t = nc.dram_tensor("wq_t", [H, 768], F32, kind="ExternalInput").ap()
    wo_t = nc.dram_tensor("wo_t", [H, H], F32, kind="ExternalInput").ap()
    wf_t = nc.dram_tensor("wf_t", [H, H], F32, kind="ExternalInput").ap()
    fb_in = nc.dram_tensor("fb", [128, 16], F32, kind="ExternalInput").ap()
    gidx = nc.dram_tensor("gidx", [128, 16, 8], I32, kind="ExternalInput").ap()
    y_t = nc.dram_tensor("y_t", [H, CH], F32, kind="ExternalOutput").ap()

    xn_d = nc.dram_tensor("xn_d", [CH, H], BF16, kind="Internal").ap()
    xn_full = nc.dram_tensor("xn_full", [TOK, H], BF16, kind="Internal").ap()
    g_d = nc.dram_tensor("g_d", [H, CH], BF16, kind="Internal").ap()
    at_d = nc.dram_tensor("at_d", [TOK, 256], BF16, kind="Internal").ap()
    at_all = nc.dram_tensor("at_all", [2 * W * S, 256], BF16, kind="Internal").ap()
    fl_in = nc.dram_tensor("fl_in", [128, 4], I32, kind="Internal").ap()
    fl_out = nc.dram_tensor("fl_out", [128, 4], I32, kind="Internal").ap()

    kb = KB(nc)
    st = ExitStack()

    dS = kb.mksem(st, "dS")
    dC = kb.mksem(st, "dC")
    dG = kb.mksem(st, "dG")
    cc = kb.mksem(st, "cc")
    pT = kb.mksem(st, "pT")
    pV = kb.mksem(st, "pV")
    pC = kb.mksem(st, "pC")
    pP = kb.mksem(st, "pP")
    bar = kb.mksem(st, "bar")

    idb = st.enter_context(nc.sbuf_tensor("idb", [128, 128], BF16))
    onesb = st.enter_context(nc.sbuf_tensor("onesb", [128, 128], BF16))
    idio = st.enter_context(nc.sbuf_tensor("idio", [128, 128], F32))
    epsb = st.enter_context(nc.sbuf_tensor("epsb", [128, 1], F32))
    gix = st.enter_context(nc.sbuf_tensor("gix", [128, 16, 8], I32))
    fbs = st.enter_context(nc.sbuf_tensor("fbs", [128, 16], F32))

    t_gix = kb.dma("sync", gix[:], gidx, dS)
    kb.dma("sync", fbs[:], fb_in, dS)
    kb.op("gpsimd", lambda e: e.iota(
        idio[:], pattern=[[1, 128]], base=0, channel_multiplier=-1,
        allow_small_or_imprecise_dtypes=True), mark=pP)
    kb.wait("vector", pP, pP["n"])
    kb.op("vector", lambda e: e.tensor_single_scalar(idb[:], idio[:], 0.0, ALU.is_equal))
    kb.op("vector", lambda e: e.memset(onesb[:], 1.0))
    m_eps = kb.op("vector", lambda e: e.memset(epsb[:], 1e-5), mark=pV)
    kb.wait("sync", pV, m_eps)
    kb.dma("sync", fl_in, gix.rearrange("p a b -> p (a b)")[:, 0:4], dS)

    p3 = ExitStack()
    wqkvT = p3.enter_context(nc.sbuf_tensor("wqkvT", [128, 16, 768], BF16))
    t_wq = kb.dma("gpsimd", wqkvT[:],
                  wq_t.rearrange("(ic p) o -> p ic o", p=128), dG)

    lnbufs = p3.enter_context(nc.sbuf_tensor("xof", [128, 4, H], BF16))
    xnb = p3.enter_context(nc.sbuf_tensor("xnb", [128, 2, H], BF16))
    bnst = p3.enter_context(nc.sbuf_tensor("bnst", [128, 24], F32))
    bnagp = p3.enter_context(nc.sbuf_tensor("bnagp", [128, 2, 2], F32))
    lnstd = p3.enter_context(nc.sbuf_tensor("lnstd", [128, 2], F32))
    lnr = p3.enter_context(nc.sbuf_tensor("lnr", [128, 2], F32))

    ln_state = {"m_sq": [0, 0], "m_aff2": [0, 0], "m_aff4": [0, 0, 0, 0],
                "xn_dma": [0, 0], "t_x": [0, 0, 0, 0]}
    xn_tile_w = [0] * 128
    xnd_tile_w = [0] * 16

    ln_jobs = ([("g", g) for g in range(64)] + [("o", t) for t in range(16)]
               + [("g", g) for g in range(64, 128)])

    def ln_load(t):
        kind, idx = ln_jobs[t]
        p4 = t % 4
        src = (x_in if kind == "g" else x_own)[idx * 128:(idx + 2) * 128, :]
        kb.wait("gpsimd", pV, ln_state["m_aff4"][p4])
        kb.wait("gpsimd", pV, ln_state["m_aff4"][p4 + 1])
        tx = kb.dma("gpsimd", lnbufs[:, p4:p4 + 2, :],
                    src.rearrange("(tt p) n -> p tt n", p=128), dG)
        ln_state["t_x"][p4] = tx
        ln_state["t_x"][p4 + 1] = tx

    def ln_stats(t):
        p4, p2 = t % 4, t % 2
        kb.wait("vector", dG, ln_state["t_x"][p4])
        kb.wait("vector", pV, ln_state["m_aff2"][p2])
        for c4 in range(3):
            kb.op("vector", lambda e, p4=p4, c4=c4: e.bn_stats(
                bnst[:, c4 * 6:(c4 + 1) * 6],
                lnbufs[:, p4, c4 * 512:(c4 + 1) * 512]))
        m_st = kb.op("vector", lambda e, p4=p4: e.bn_stats(
            bnst[:, 18:24], lnbufs[:, p4, 1536:2048]), mark=pV)
        kb.wait("vector", pV, m_st)
        m_bn = kb.op("vector", lambda e, p2=p2: e.bn_aggr(
            bnagp[:, p2, :], bnst[:]), mark=pV)
        kb.wait("scalar", pV, m_bn)
        ln_state["m_sq"][p2] = kb.op("scalar", lambda e, p2=p2: e.activation(
            lnstd[:, p2:p2 + 1], bnagp[:, p2, 1:2], AF.Sqrt,
            bias=epsb[:], scale=1.0), mark=pC)

    def ln_fin(t):
        kind, idx = ln_jobs[t]
        p4, p2 = t % 4, t % 2
        dst = (xn_full if kind == "g" else xn_d)[idx * 128:(idx + 1) * 128, :]
        kb.wait("vector", pC, ln_state["m_sq"][p2])
        m_rc = kb.op("vector", lambda e, p2=p2: e.reciprocal(
            lnr[:, p2:p2 + 1], lnstd[:, p2:p2 + 1]), mark=pV)
        kb.wait("vector", pV, m_rc)
        kb.wait("vector", dG, ln_state["xn_dma"][p2])
        m_aff = kb.op("vector", lambda e, p4=p4, p2=p2: e.tensor_scalar(
            xnb[:, p2, :], lnbufs[:, p4, :], bnagp[:, p2, 0:1],
            lnr[:, p2:p2 + 1], ALU.subtract, ALU.mult), mark=pV)
        ln_state["m_aff2"][p2] = m_aff
        ln_state["m_aff4"][p4] = m_aff
        kb.wait("gpsimd", pV, m_aff)
        t_w = kb.dma("gpsimd", dst, xnb[:, p2, :], dG)
        ln_state["xn_dma"][p2] = t_w
        if kind == "g":
            xn_tile_w[idx] = t_w
        else:
            xnd_tile_w[idx] = t_w

    ln_ld = [0]
    ln_st = [0]
    ln_done = [0]
    NJOBS = len(ln_jobs)

    def pump_ln(k):
        while k > 0 and ln_done[0] < NJOBS:
            if ln_ld[0] < NJOBS and ln_ld[0] - ln_done[0] <= 2:
                ln_load(ln_ld[0]); ln_ld[0] += 2
            if ln_st[0] < ln_ld[0] and (ln_ld[0] - ln_st[0] > 1
                                        or ln_ld[0] == NJOBS):
                ln_stats(ln_st[0]); ln_st[0] += 1
            if ln_done[0] < ln_st[0] and (ln_st[0] - ln_done[0] > 0
                                          and (ln_ld[0] - ln_done[0] > 2
                                               or ln_ld[0] == NJOBS)):
                ln_fin(ln_done[0]); ln_done[0] += 1
            k -= 1

    kb.wait("scalar", pV, m_eps)
    pump_ln(10)

    stage = p3.enter_context(nc.sbuf_tensor("stage", [128, 64, 128], BF16))
    xnT = p3.enter_context(nc.sbuf_tensor("xnT", [128, 2, 16, 1024], BF16))
    qth = p3.enter_context(nc.sbuf_tensor("qth", [128, S], BF16))
    kth = p3.enter_context(nc.sbuf_tensor("kth", [128, S], BF16))
    vtih = p3.enter_context(nc.sbuf_tensor("vtih", [128, S], BF16))
    vth = p3.enter_context(nc.sbuf_tensor("vth", [128, 64, 132], BF16))
    Ebuf = p3.enter_context(nc.sbuf_tensor("Ebuf", [128, 2, 1024], BF16))
    sqs = Ebuf.rearrange("p a b -> p (a b)")
    rq = p3.enter_context(nc.sbuf_tensor("rq", [128, 1], F32))
    rk = p3.enter_context(nc.sbuf_tensor("rk", [128, 1], F32))
    rstat = p3.enter_context(nc.sbuf_tensor("rstat", [128, 8], F32))
    negc = p3.enter_context(nc.sbuf_tensor("negc", [128, 1], F32))
    rlq = p3.enter_context(nc.sbuf_tensor("rlq", [128, 4], F32))

    kb.op("vector", lambda e: e.memset(vth[:, :, 128:132], 0.0))
    m_vones = kb.op("vector", lambda e: e.memset(vth[:, :, 128:129], 1.0), mark=pV)

    xnT_rd = [0, 0]
    stage_dma = 0
    st_wr = 0
    ops_free = 0
    exp_done = [0, 0]
    pv_done_for_E = [0, 0]

    for b in range(B):
        for hl in range(2):
            with ExitStack() as pq:
                qkvps = pq.enter_context(
                    nc.psum_tensor(f"qkvps{b}_{hl}", [128, 2048], F32))
                kb.wait("tensor", pV, ops_free)
                ev_done = [0, 0]
                for m in range(8):
                    par = m % 2
                    if b == 0:
                        pump_ln(8 if hl == 0 else 3)
                    need = xn_tile_w[b * 64 + m * 8 + 7]
                    assert need > 0, f"LN tile for mega b{b} m{m} not scheduled"
                    kb.wait("sync", pT, xnT_rd[par])
                    kb.wait("sync", dG, need)
                    r0 = b * S + m * 1024
                    for ic in range(16):
                        t_ld = kb.dma(
                            "sync", xnT[:, par, ic, :],
                            xn_full[r0:r0 + 1024, ic * 128:(ic + 1) * 128],
                            dS, transpose=True)
                    if b == 0 and hl == 0 and m == 0:
                        kb.wait("tensor", dG, t_wq)
                    kb.wait("tensor", dS, t_ld)
                    for j, oc in enumerate((hl, 2 + hl, 4 + hl)):
                        pp = j % 2
                        kb.wait("tensor", pV, ev_done[pp])
                        for ic in range(16):
                            for tb in range(2):
                                fn = lambda e, ic=ic, tb=tb, par=par, pp=pp, oc=oc: \
                                    e.matmul(
                                        qkvps[:, pp * 1024 + tb * 512:
                                              pp * 1024 + (tb + 1) * 512],
                                        wqkvT[:, ic, oc * 128:(oc + 1) * 128],
                                        xnT[:, par, ic, tb * 512:(tb + 1) * 512],
                                        start=(ic == 0), stop=(ic == 15))
                                if ic == 15 and tb == 1:
                                    m_mm = kb.op("tensor", fn, mark=pT)
                                else:
                                    kb.op("tensor", fn)
                        if j == 2:
                            xnT_rd[par] = m_mm
                        dst = (qth, kth, vtih)[j]
                        kb.wait("vector", pT, m_mm)
                        ev_done[pp] = kb.op(
                            "vector",
                            lambda e, dst=dst, m=m, pp=pp: e.tensor_copy(
                                dst[:, m * 1024:(m + 1) * 1024],
                                qkvps[:, pp * 1024:(pp + 1) * 1024]),
                            mark=pV)
                m_qkv_ev = pV["n"]

            with ExitStack() as pv_:
                tpsV = pv_.enter_context(
                    nc.psum_tensor(f"tpsV{b}_{hl}", [128, 2, 1024], BF16))
                kb.wait("tensor", pV, m_qkv_ev)
                if b == 0 and hl == 0:
                    kb.wait("tensor", pV, m_vones)
                tr_ev = [0, 0]
                for grp in range(16):
                    gp = grp % 2
                    kb.wait("tensor", pV, tr_ev[gp])
                    for i in range(4):
                        fn = lambda e, grp=grp, i=i, gp=gp: e.transpose(
                            tpsV[:, gp, i * 128:(i + 1) * 128],
                            vtih[:, (grp * 4 + i) * 128:(grp * 4 + i + 1) * 128],
                            idb[:])
                        if i == 3:
                            m_tr = kb.op("tensor", fn, mark=pT)
                        else:
                            kb.op("tensor", fn)
                    kb.wait("vector", pT, m_tr)
                    tr_ev[gp] = kb.op(
                        "vector",
                        lambda e, grp=grp, gp=gp: e.tensor_copy(
                            vth[:, grp * 4:(grp + 1) * 4, 0:128],
                            tpsV[:, gp, 0:512].rearrange("p (k d) -> p k d", k=4)),
                        mark=pV)
                m_vt = pV["n"]

            with ExitStack() as pa:
                sps = pa.enter_context(
                    nc.psum_tensor(f"sps{b}_{hl}", [128, 2, 1024], F32))
                ops = [pa.enter_context(
                    nc.psum_tensor(f"ops{b}_{hl}_{i}", [128, 512], F32))
                    for i in range(4)]

                kb.wait("tensor", pV, m_vt)
                kb.wait("scalar", pV, m_qkv_ev)
                kb.wait("scalar", pT, max(pv_done_for_E))
                m_red = 0
                spsf = sps.rearrange("p a b -> p (a b)")
                for which, src in enumerate((qth, kth)):
                    for chn in range(4):
                        m_sq2 = kb.op("scalar", lambda e, src=src, chn=chn: e.activation(
                            sqs[:], src[:, chn * 2048:(chn + 1) * 2048],
                            AF.Square), mark=pC)
                        kb.wait("tensor", pC, m_sq2)
                        kb.wait("tensor", pV, m_red)
                        for c4 in range(4):
                            fn = lambda e, c4=c4: e.matmul(
                                spsf[:, c4 * 512:(c4 + 1) * 512], onesb[:],
                                sqs[:, c4 * 512:(c4 + 1) * 512],
                                start=True, stop=True)
                            if c4 == 3:
                                m_nq = kb.op("tensor", fn, mark=pT)
                            else:
                                kb.op("tensor", fn)
                        kb.wait("vector", pT, m_nq)
                        idx = which * 4 + chn
                        m_red = kb.op("vector", lambda e, idx=idx: e.reduce_max(
                            rstat[:, idx:idx + 1], spsf[:], axis=AX.X), mark=pV)
                        kb.wait("scalar", pT, m_nq)
                kb.wait("vector", pV, m_red)
                kb.op("vector", lambda e: e.reduce_max(rq[:], rstat[:, 0:4], axis=AX.X))
                m_rk = kb.op("vector", lambda e: e.reduce_max(
                    rk[:], rstat[:, 4:8], axis=AX.X), mark=pV)
                kb.wait("vector", pV, m_rk)
                m_prod = kb.op("vector", lambda e: e.tensor_mul(rq[:], rq[:], rk[:]),
                               mark=pV)
                kb.wait("scalar", pV, m_prod)
                m_sqc = kb.op("scalar", lambda e: e.activation(
                    negc[:], rq[:], AF.Sqrt, scale=CMARG / 128.0), mark=pC)
                kb.wait("vector", pC, m_sqc)
                m_negc = kb.op("vector", lambda e: e.tensor_scalar_mul(
                    negc[:], negc[:], -1.0), mark=pV)
                kb.wait("scalar", pV, m_negc)

                kb.wait("tensor", pV, m_red)
                pend = [None]
                pv_last = {}
                norm_done = {}

                def emit_norm(qb2, hl=hl):
                    kb.wait("vector", pT, pv_last[qb2])
                    if qb2 == 0:
                        kb.wait("vector", dC, stage_dma)
                    for sub in range(3):
                        kb.op("vector", lambda e, sub=sub: e.reciprocal(
                            rlq[:, sub:sub + 1], ops[sub][:, 128:129]))
                    m_rl = kb.op("vector", lambda e: e.reciprocal(
                        rlq[:, 3:4], ops[3][:, 128:129]), mark=pV)
                    kb.wait("vector", pV, m_rl)
                    m_n = None
                    for sub in range(4):
                        fn = lambda e, sub=sub, qb2=qb2: e.tensor_scalar(
                            stage[:, qb2 * 4 + sub, :],
                            ops[sub][:, 0:128], rlq[:, sub:sub + 1], None, ALU.mult)
                        if sub == 3:
                            m_n = kb.op("vector", fn, mark=pV)
                        else:
                            kb.op("vector", fn)
                    norm_done[qb2] = m_n
                    return m_n

                for qb in range(16):
                    if b == 0:
                        pump_ln(2)
                    ng2 = 2 * (qb + 1)

                    def emit_S(g2, qb=qb):
                        epar = g2 % 2
                        kb.wait("tensor", pC, exp_done[epar])
                        for i in range(2):
                            kc = 2 * g2 + i
                            fn = lambda e, kc=kc, i=i, qb=qb, epar=epar: e.matmul(
                                sps[:, epar, i * 512:(i + 1) * 512],
                                kth[:, kc * 128:(kc + 1) * 128],
                                qth[:, qb * 512:(qb + 1) * 512],
                                start=True, stop=True)
                            if i == 1:
                                return kb.op("tensor", fn, mark=pT)
                            kb.op("tensor", fn)

                    def emit_exp(g2, m_s, qb=qb):
                        epar = g2 % 2
                        diag = (g2 >= 2 * (qb + 1) - 2)
                        kb.wait("scalar", pT, m_s)
                        kb.wait("scalar", pT, pv_done_for_E[epar])
                        m_exp = kb.op("scalar", lambda e, epar=epar: e.activation(
                            Ebuf[:, epar, :], sps[:, epar, :], AF.Exp,
                            bias=negc[:], scale=SCALE), mark=pC)
                        exp_done[epar] = m_exp
                        m_msk = 0
                        if diag:
                            kb.wait("gpsimd", pC, m_exp)
                            for i in range(2):
                                kc = 2 * g2 + i
                                ii = kc - 4 * qb
                                fn = lambda e, epar=epar, i=i, ii=ii: e.affine_select(
                                    Ebuf[:, epar, i * 512:(i + 1) * 512],
                                    Ebuf[:, epar, i * 512:(i + 1) * 512],
                                    pattern=[[1, 512]], compare_op=ALU.is_ge,
                                    fill=0.0, base=-128 * ii, channel_multiplier=-1)
                                if i == 1:
                                    m_msk = kb.op("gpsimd", fn, mark=pP)
                                else:
                                    kb.op("gpsimd", fn)
                        return m_exp, m_msk

                    def emit_PV(qb2, g2, m_exp, m_msk):
                        epar = g2 % 2
                        ng2i = 2 * (qb2 + 1)
                        kb.wait("tensor", pC, m_exp)
                        if m_msk:
                            kb.wait("tensor", pP, m_msk)
                        if g2 == 0:
                            kb.wait("tensor", pV, norm_done[qb2 - 1] if qb2 > 0
                                    else ops_free)
                        m_pv = None
                        for i in range(2):
                            for sub in range(4):
                                fn = lambda e, epar=epar, i=i, sub=sub, g2=g2,                                     ng2i=ng2i: e.matmul(
                                        ops[sub][:, 0:129],
                                        Ebuf[:, epar, i * 512 + sub * 128:
                                             i * 512 + (sub + 1) * 128],
                                        vth[:, 2 * g2 + i, 0:129],
                                        start=(g2 == 0 and i == 0),
                                        stop=(g2 == ng2i - 1 and i == 1))
                                if i == 1 and sub == 3:
                                    m_pv = kb.op("tensor", fn, mark=pT)
                                else:
                                    kb.op("tensor", fn)
                        pv_done_for_E[epar] = m_pv
                        pv_last[qb2] = m_pv
                        return m_pv

                    for g2 in range(ng2):
                        m_s = emit_S(g2)
                        if pend[0] is not None:
                            emit_PV(*pend[0])
                        pend[0] = (qb, g2) + emit_exp(g2, m_s)
                        if qb > 0 and g2 == 0 and (qb - 1) in pv_last                            and (qb - 1) not in norm_done:
                            emit_norm(qb - 1)

                if pend[0] is not None:
                    emit_PV(*pend[0])
                    pend[0] = None
                emit_norm(15)
                ops_free = norm_done[15]
                st_wr = norm_done[15]

            kb.wait("scalar", pV, st_wr)
            stage_dma = kb.dma(
                "scalar",
                at_d[b * S:(b + 1) * S, hl * 128:(hl + 1) * 128].rearrange(
                    "(p sub) f -> p sub f", p=128),
                stage[:], dC)

        kb.wait("gpsimd", dC, stage_dma)
        kb.op("gpsimd", lambda e, b=b: e.collective_compute(
            "AllGather", ALU.bypass, replica_groups=[list(range(W))],
            ins=[at_d[b * S:(b + 1) * S, :].opt()],
            outs=[at_all[b * W * S:(b + 1) * W * S, :].opt()]), mark=cc)

    assert ln_done[0] == NJOBS, f"LN jobs left: {NJOBS - ln_done[0]}"

    kb.op("gpsimd", lambda e: e.collective_compute(
        "AllReduce", ALU.add, replica_groups=[list(range(W))],
        ins=[fl_in.opt()], outs=[fl_out.opt()]), mark=cc)
    cc_at = cc["n"]

    kb.barrier(bar)
    p3.close()

    attnT = st.enter_context(nc.sbuf_tensor("attnT", [128, 16, 2048], BF16))
    gT = st.enter_context(nc.sbuf_tensor("gT", [128, 2, 2048], BF16))
    alpear = st.enter_context(nc.sbuf_tensor("alpear", [128, 512], F32))
    p5a = ExitStack()
    xnTo = p5a.enter_context(nc.sbuf_tensor("xnTo", [128, 16, 2048], BF16))
    awf = p5a.enter_context(nc.sbuf_tensor("awf", [128, 16, 512], BF16))
    wfc = p5a.enter_context(nc.sbuf_tensor("wfc", [128, 16, 512], F32))

    kb.wait("sync", dG, max(xnd_tile_w))
    t_xnTo = 0
    for ic in range(16):
        for half in range(2):
            t_xnTo = kb.dma(
                "sync", xnTo[:, ic, half * 1024:(half + 1) * 1024],
                xn_d[half * 1024:(half + 1) * 1024, ic * 128:(ic + 1) * 128],
                dS, transpose=True)
    kb.wait("tensor", dS, t_xnTo)

    silu_done = 0
    gT_dma = [0, 0]
    m_bw = 0
    m_ffn = 0
    t_wfq = [0] * 4
    t_wfq[0] = kb.dma("sync", wfc[:],
                      wf_t[:, 0:512].rearrange("(ic p) o -> p ic o", p=128), dS)
    for qu in range(4):
        with ExitStack() as pw:
            kb.wait("scalar", dS, t_wfq[qu])
            if qu > 0:
                kb.wait("scalar", pT, m_ffn)
            m_abs = None
            for ic in range(16):
                m_abs = kb.op("scalar", lambda e, ic=ic: e.activation(
                    awf[:, ic, :], wfc[:, ic, :], AF.Abs), mark=pC)
            with ExitStack() as palp:
                aps = palp.enter_context(nc.psum_tensor(f"aps{qu}", [128, 512], F32))
                kb.wait("tensor", pC, m_abs)
                if qu > 0:
                    kb.wait("tensor", pC, silu_done)
                for ic in range(16):
                    fn = lambda e, ic=ic: e.matmul(
                        aps[:], onesb[:], awf[:, ic, :],
                        start=(ic == 0), stop=(ic == 15))
                    if ic == 15:
                        m_aps = kb.op("tensor", fn, mark=pT)
                    else:
                        kb.op("tensor", fn)
                kb.wait("vector", pT, m_aps)
                if qu > 0:
                    kb.wait("vector", pV, m_bw)
                m_alp = kb.op("vector", lambda e: e.tensor_scalar_mul(
                    alpear[:], aps[:], 1.0 / 2048.0), mark=pV)
            kb.wait("vector", pV, m_alp)
            for ic in range(16):
                kb.op("vector", lambda e, ic=ic: e.tensor_single_scalar(
                    awf[:, ic, :], wfc[:, ic, :], 0.0, ALU.is_ge))
                kb.op("vector", lambda e, ic=ic: e.tensor_scalar(
                    awf[:, ic, :], awf[:, ic, :], 2.0, 1.0,
                    ALU.mult, ALU.subtract))
                m_bw = kb.op("vector", lambda e, ic=ic: e.tensor_mul(
                    awf[:, ic, :], awf[:, ic, :], alpear[:]), mark=pV)
            if qu < 3:
                kb.wait("sync", pV, m_bw)
                kb.wait("sync", pC, m_abs)
                t_wfq[qu + 1] = kb.dma(
                    "sync", wfc[:],
                    wf_t[:, (qu + 1) * 512:(qu + 2) * 512].rearrange(
                        "(ic p) o -> p ic o", p=128), dS)
        with ExitStack() as pps:
            gps = pps.enter_context(nc.psum_tensor(f"gps{qu}", [128, 2048], F32))
            for fl in range(4):
                fc = qu * 4 + fl
                gpar = fc % 2
                kb.wait("tensor", pC, silu_done)
                if fl == 0:
                    kb.wait("tensor", pV, m_bw)
                for ic in range(16):
                    for tb in range(4):
                        fn = lambda e, ic=ic, fl=fl, tb=tb: e.matmul(
                            gps[:, tb * 512:(tb + 1) * 512],
                            awf[:, ic, fl * 128:(fl + 1) * 128],
                            xnTo[:, ic, tb * 512:(tb + 1) * 512],
                            start=(ic == 0), stop=(ic == 15))
                        if ic == 15 and tb == 3:
                            m_ffn = kb.op("tensor", fn, mark=pT)
                        else:
                            kb.op("tensor", fn)
                kb.wait("scalar", pT, m_ffn)
                kb.wait("scalar", dS, gT_dma[gpar])
                silu_done = kb.op("scalar", lambda e, fc=fc, gpar=gpar: e.activation(
                    gT[:, gpar, :], gps[:], AF.Silu,
                    bias=fbs[:, fc:fc + 1], scale=1.0), mark=pC)
                kb.wait("sync", pC, silu_done)
                gT_dma[gpar] = kb.dma(
                    "sync", g_d[fc * 128:(fc + 1) * 128, :], gT[:, gpar, :], dS)
    p5a.close()

    with ExitStack() as pg:
        gbuf = pg.enter_context(nc.sbuf_tensor("gbuf", [128, 16, 256], BF16))
        tpsC = pg.enter_context(nc.psum_tensor("tpsC", [128, 2, 1024], BF16))
        kb.wait("gpsimd", cc, cc_at)
        kb.wait("gpsimd", dS, t_gix)
        kb.wait("gpsimd", pT, m_ffn)
        kb.wait("gpsimd", pC, silu_done)
        kb.wait("tensor", pC, silu_done)
        tr_rd = [0] * 16
        ps_rd = [0, 0]
        it = 0
        for j in range(16):
            for r in range(8):
                sl = it % 16
                par = it % 2
                kb.wait("gpsimd", pV, tr_rd[sl])
                t_g = kb.idma(gbuf[:, sl, :], at_all, gix[:, j, r:r + 1], dG)
                kb.wait("tensor", dG, t_g)
                kb.wait("tensor", pV, ps_rd[par])
                for u in range(2):
                    fn = lambda e, sl=sl, par=par, u=u: e.transpose(
                        tpsC[:, par, u * 128:(u + 1) * 128],
                        gbuf[:, sl, u * 128:(u + 1) * 128], idb[:])
                    if u == 1:
                        m_trc = kb.op("tensor", fn, mark=pT)
                    else:
                        kb.op("tensor", fn)
                kb.wait("vector", pT, m_trc)
                m_ev = kb.op("vector", lambda e, j=j, r=r, par=par: e.tensor_copy(
                    attnT[:, 2 * r:2 * r + 2, j * 128:(j + 1) * 128],
                    tpsC[:, par, 0:256].rearrange("p (k d) -> p k d", k=2)),
                    mark=pV)
                tr_rd[sl] = m_ev
                ps_rd[par] = m_ev
                it += 1
        m_attnT = pV["n"]

    xoT = st.enter_context(nc.sbuf_tensor("xoT", [128, 2, 2048], F32))
    ystg = st.enter_context(nc.sbuf_tensor("ystg", [128, 2, 2048], F32))
    gTi = st.enter_context(nc.sbuf_tensor("gTi", [128, 2, 2048], BF16))
    woqf = st.enter_context(nc.sbuf_tensor("woqf", [128, 16, 512], F32))
    woq = st.enter_context(nc.sbuf_tensor("woq", [128, 2, 16, 512], BF16))

    kb.wait("tensor", pV, m_attnT)
    kb.wait("sync", pV, m_attnT)
    kb.wait("scalar", pV, m_attnT)
    kb.wait("gpsimd", pV, m_attnT)

    y_dma = 0
    m_s2 = 0
    m_s2_q = {}
    cast_pP = {}
    t_wof = {}
    gTi_rd = [0, 0]
    y_dma_par = [0, 0]
    xo_rd = [(pV, 0), (pV, 0)]

    def emit_wload(qu):
        if qu >= 4:
            return
        if qu > 0:
            kb.wait("scalar", pP, cast_pP[qu - 1])
        t_wof[qu] = kb.dma("scalar", woqf[:],
                           wo_t[:, qu * 512:(qu + 1) * 512].rearrange(
                               "(ic p) o -> p ic o", p=128), dC)

    def emit_wcast(qu):
        if qu >= 4:
            return
        kb.wait("gpsimd", dC, t_wof[qu])
        if qu >= 2:
            kb.wait("gpsimd", pT, m_s2_q[qu - 2])
        m_c = None
        for ic in range(16):
            fn = lambda e, qu=qu, ic=ic: e.tensor_copy(
                woq[:, qu % 2, ic, :], woqf[:, ic, :])
            if ic == 15:
                m_c = kb.op("gpsimd", fn, mark=pP)
            else:
                kb.op("gpsimd", fn)
        cast_pP[qu] = m_c

    emit_wload(0)
    emit_wcast(0)
    emit_wload(1)
    with ExitStack() as pps:
        sps2ab = [pps.enter_context(
            nc.psum_tensor(f"sps2_{i}", [128, 2048], F32))
            for i in range(2)]
        for qu in range(4):
            for fl in range(4):
                fc = qu * 4 + fl
                par = fc % 2
                sps2 = sps2ab[par][:]
                kb.wait("sync", *xo_rd[par])
                xo_dma = kb.dma("sync", xoT[:, par, :],
                                x_own_t[fc * 128:(fc + 1) * 128, :], dS)
                kb.wait("sync", pV, gTi_rd[par])
                gt_dma = kb.dma("sync", gTi[:, par, :],
                                g_d[fc * 128:(fc + 1) * 128, :], dS)
                kb.wait("tensor", pV, gTi_rd[par])
                if fl == 0:
                    kb.wait("tensor", pP, cast_pP[qu])
                for ic in range(16):
                    for tb in range(4):
                        fn = lambda e, ic=ic, fl=fl, qu=qu, tb=tb, sps2=sps2: \
                            e.matmul(
                                sps2[:, tb * 512:(tb + 1) * 512],
                                woq[:, qu % 2, ic, fl * 128:(fl + 1) * 128],
                                attnT[:, ic, tb * 512:(tb + 1) * 512],
                                start=(ic == 0), stop=(ic == 15))
                        if ic == 15 and tb == 3:
                            m_s2 = kb.op("tensor", fn, mark=pT)
                        else:
                            kb.op("tensor", fn)
                if fl == 3:
                    m_s2_q[qu] = m_s2
                if fl == 0:
                    emit_wcast(qu + 1)
                    emit_wload(qu + 2)
                kb.wait("vector", pT, m_s2)
                kb.wait("vector", dS, gt_dma)
                kb.wait("vector", dS, y_dma_par[par])
                m_mult = kb.op(
                    "vector", lambda e, par=par, sps2=sps2: e.scalar_tensor_tensor(
                        ystg[:, par, :], sps2[:], 1.0, gTi[:, par, :],
                        ALU.bypass, ALU.mult), mark=pV)
                gTi_rd[par] = m_mult
                if fc % 2 == 0:
                    kb.wait("vector", dS, xo_dma)
                    m_add = kb.op("vector", lambda e, par=par: e.tensor_add(
                        ystg[:, par, :], ystg[:, par, :], xoT[:, par, :]),
                        mark=pV)
                    add_tok = (pV, m_add)
                else:
                    kb.wait("gpsimd", pV, m_mult)
                    kb.wait("gpsimd", dS, xo_dma)
                    m_add = kb.op("gpsimd", lambda e, par=par: e.tensor_add(
                        ystg[:, par, :], ystg[:, par, :], xoT[:, par, :]),
                        mark=pP)
                    add_tok = (pP, m_add)
                xo_rd[par] = add_tok
                kb.wait("sync", *add_tok)
                y_dma = kb.dma("sync", y_t[fc * 128:(fc + 1) * 128, :],
                               ystg[:, par, :], dS)
                y_dma_par[par] = y_dma

    if debug:
        o_attnT = nc.dram_tensor("o_attnT", [128, 16, 2048], BF16,
                                 kind="ExternalOutput").ap()
        o_atd = nc.dram_tensor("o_atd", [TOK, 256], BF16, kind="ExternalOutput").ap()
        o_gd = nc.dram_tensor("o_gd", [H, CH], BF16, kind="ExternalOutput").ap()
        o_xnd = nc.dram_tensor("o_xnd", [CH, H], BF16, kind="ExternalOutput").ap()
        o_xnf = nc.dram_tensor("o_xnf", [TOK, H], BF16, kind="ExternalOutput").ap()
        kb.wait("sync", dS, y_dma)
        kb.dma("sync", o_attnT, attnT[:], dS)
        kb.dma("sync", o_gd, g_d, dS)
        kb.dma("sync", o_xnd, xn_d, dS)
        kb.dma("sync", o_xnf, xn_full, dS)
        y_dma = kb.dma("sync", o_atd, at_d, dS)

    kb.wait("sync", dS, y_dma)

    with nc.Block() as blk:
        kb.emit(blk)
    st.close()
    return nc


_NC_CACHE = None


def _get_nc():
    global _NC_CACHE
    if _NC_CACHE is None:
        _NC_CACHE = build()
    return _NC_CACHE


def make_in_maps(L0_attn_hidden_states, L0_attn_w_qkv, L0_attn_w_o, L0_ffn_weight,
                 L0_ffn_bias):
    x = np.ascontiguousarray(np.asarray(L0_attn_hidden_states, dtype=np.float32)
                             ).reshape(TOK, H)
    wqkv = np.asarray(L0_attn_w_qkv, dtype=np.float32)
    wo = np.asarray(L0_attn_w_o, dtype=np.float32)
    wf = np.asarray(L0_ffn_weight, dtype=np.float32)
    fbv = np.asarray(L0_ffn_bias, dtype=np.float32)

    wo_t = np.ascontiguousarray(wo.T)
    wf_t = np.ascontiguousarray(wf.T)
    fb_l = np.ascontiguousarray(fbv.reshape(16, 128).T)

    in_maps = []
    for c in range(W):
        sl = np.concatenate([
            wqkv[256 * c:256 * c + 256],
            wqkv[H + 256 * c:H + 256 * c + 256],
            wqkv[2 * H + 256 * c:2 * H + 256 * c + 256],
        ], axis=0)
        wq_tc = np.ascontiguousarray(sl.T)
        x_own = np.ascontiguousarray(x[c * CH:(c + 1) * CH])
        x_own_t = np.ascontiguousarray(x_own.T)
        gi = np.zeros((128, 16, 8), dtype=np.int32)
        p = np.arange(128)
        for j in range(16):
            g = c * CH + 128 * j + p
            bb = g // S
            s = g % S
            sub = s // 128
            pp = s % 128
            row = pp * 64 + sub
            for r in range(8):
                gi[:, j, r] = bb * W * S + r * S + row
        in_maps.append({
            "x": x, "x_own": x_own, "x_own_t": x_own_t, "wq_t": wq_tc,
            "wo_t": wo_t, "wf_t": wf_t, "fb": fb_l, "gidx": gi,
        })
    return in_maps


def kernel(L0_attn_hidden_states, L0_attn_w_qkv, L0_attn_w_o, L0_ffn_weight,
           L0_ffn_bias):
    in_maps = make_in_maps(L0_attn_hidden_states, L0_attn_w_qkv, L0_attn_w_o,
                           L0_ffn_weight, L0_ffn_bias)
    nc = _get_nc()
    res = run_bass_kernel_spmd(nc, in_maps, core_ids=list(range(W)))
    y = np.concatenate([np.asarray(res.results[c]["y_t"]).T for c in range(W)],
                       axis=0)
    return np.ascontiguousarray(y.reshape(B, S, H))
